# version 17 (final)
# baseline (speedup 1.0000x reference)
import contextlib

import numpy as np

import concourse.bass as bass
import concourse.mybir as mybir
from concourse.bass_utils import run_bass_kernel_spmd

F32 = mybir.dt.float32
BF16 = mybir.dt.bfloat16
AF = mybir.ActivationFunctionType
ALU = mybir.AluOpType
AXX = mybir.AxisListType.X

S = 2
T = 4096
D = 1024
NCH = 8
HD = 16
HL = 8
NW = 8
HID = 2730
HIDP = 2816
NF1 = 44
NI2 = 22
LN_EPS = 1e-5
NEG = -1.0e30
LN1E4 = float(np.log(10000.0))
TWO_PI = float(2.0 * np.pi)
MAGIC = 12582912.0

COMPUTE_SEMS = {"tensor", "vector", "scalar", "gpsimdc"}


def _flat(waits):
    o = []
    for w in waits:
        if w is None:
            continue
        if isinstance(w, list):
            o.extend(x for x in w if x is not None)
        else:
            o.append(w)
    return o


class Bld:
    ENGS = ("sync", "tensor", "vector", "scalar", "gpsimd")

    def __init__(self, nc, es):
        self.nc = nc
        self.es = es
        self.ops = {e: [] for e in self.ENGS}
        self.sem = {}
        for name in ("sync", "tensor", "vector", "scalar", "gpsimdd", "gpsimdc"):
            self.sem[name] = [es.enter_context(nc.semaphore("sem_" + name)), 0]
        self.waited = {e: {} for e in self.ENGS}

    def _dsem(self, name):
        if name not in self.sem:
            self.sem[name] = [self.es.enter_context(self.nc.semaphore("sem_" + name)), 0]
        return name

    def _semname(self, eng, is_dma):
        if eng == "gpsimd":
            return "gpsimdd" if is_dma else "gpsimdc"
        return eng

    def _resolve(self, eng, waits):
        best = {}
        for (sn, val) in _flat(waits):
            if val > best.get(sn, 0):
                best[sn] = val
        todo = []
        wd = self.waited[eng]
        for sn, val in best.items():
            if wd.get(sn, -1) >= val:
                continue
            wd[sn] = val
            todo.append((self.sem[sn][0], val))
        return todo

    SELF_SYNC = {"vector", "scalar", "gpsimdc"}

    def emit(self, eng, fn, waits=(), inc=True, is_dma=False, sem=None):
        semname = self._dsem(sem) if (is_dma and sem) else self._semname(eng, is_dma)
        amt = 16 if is_dma else 1
        if semname in self.SELF_SYNC and self.sem[semname][1] > 0:
            waits = list(waits) + [(semname, self.sem[semname][1])]
        marker = None
        incval = None
        if inc:
            self.sem[semname][1] += amt
            incval = self.sem[semname][1]
            marker = (semname, incval)
        todo = self._resolve(eng, waits)
        h = self.sem[semname][0]

        def run(e, fn=fn, todo=todo, h=h, amt=amt, incval=incval):
            for (hh, v) in todo:
                e.wait_ge(hh, v)
            instr = fn(e)
            if incval is not None:
                instr.then_inc(h, amt)

        self.ops[eng].append(run)
        return marker

    def dma(self, eng, out, in_, waits=(), sem=None, **kw):
        return self.emit(eng, lambda e: e.dma_start(out=out, in_=in_, **kw),
                         waits=waits, is_dma=True, sem=sem)

    def wait(self, eng, waits):
        todo = self._resolve(eng, waits)
        if todo:
            def run(e, todo=todo):
                for (hh, v) in todo:
                    e.wait_ge(hh, v)
            self.ops[eng].append(run)

    def now(self, semname):
        return (semname, self.sem[semname][1])

    def barrier(self):
        marks = [self.now(sn) for sn in self.sem]
        for e in self.ENGS:
            self.wait(e, marks)

    def flush(self):
        ops = self.ops
        self.ops = {e: [] for e in self.ENGS}
        with self.nc.Block() as block:
            @block.sync
            def _(e, _o=ops["sync"]):
                for f in _o:
                    f(e)

            @block.tensor
            def _(e, _o=ops["tensor"]):
                for f in _o:
                    f(e)

            @block.vector
            def _(e, _o=ops["vector"]):
                for f in _o:
                    f(e)

            @block.scalar
            def _(e, _o=ops["scalar"]):
                for f in _o:
                    f(e)

            @block.gpsimd
            def _(e, _o=ops["gpsimd"]):
                for f in _o:
                    f(e)


def build_program(dbg=(), phases=99):
    nc = bass.Bass("TRN2", target_bir_lowering=False, debug=False)
    es = contextlib.ExitStack()

    def dram(name, shape, dt=F32, kind="Internal"):
        k = "ExternalOutput" if (name in dbg and kind == "Internal") else kind
        return nc.dram_tensor(name, shape, dt, kind=k).ap()

    x0 = dram("x0", [S, T, D], F32, kind="ExternalInput")
    wqkv_d = dram("wqkv_d", [8, 128, 384], F32, kind="ExternalInput")
    wproj = dram("wproj", [8, 128, 128], F32, kind="ExternalInput")
    bprojT = dram("bprojT", [128, 8], F32, kind="ExternalInput")
    wqkv_l = dram("wqkv_l", [2, D, 1536], F32, kind="ExternalInput")
    qsT = dram("qsT", [2, 64, 1], F32, kind="ExternalInput")
    ksT = dram("ksT", [2, 64, 1], F32, kind="ExternalInput")
    qswT = dram("qswT", [2, 64, 1], F32, kind="ExternalInput")
    kswT = dram("kswT", [2, 64, 1], F32, kind="ExternalInput")
    wo_w = dram("wo_w", [2, 512, D], F32, kind="ExternalInput")
    lngT = dram("lngT", [2, 128, 8], F32, kind="ExternalInput")
    lnbT = dram("lnbT", [2, 128, 8], F32, kind="ExternalInput")
    w1p = dram("w1p", [2, D, 2 * HIDP], F32, kind="ExternalInput")
    b1pT = dram("b1pT", [2, 128, NF1], F32, kind="ExternalInput")
    w2p = dram("w2p", [2, HIDP, D], F32, kind="ExternalInput")
    b2T = dram("b2T", [2, 128, 8], F32, kind="ExternalInput")
    out = dram("out", [S, T, D], F32, kind="ExternalOutput")

    xnT = dram("xnT", [S, D, T])
    qdT = dram("qdT", [S, D, T], BF16)
    kdT = dram("kdT", [S, D, T], BF16)
    vdt = dram("vdt", [S, T, HD * 65], BF16)
    odT = dram("odT", [S, D, T], BF16)
    dnT = dram("dnT", [S, D, T])
    qlT = dram("qlT", [S, 512, T], BF16)
    klT = dram("klT", [S, 512, T], BF16)
    vlt = dram("vlt", [S, T, HL * 65], BF16)
    olT = dram("olT", [S, 512, T], BF16)
    xaT = dram("xaT", [S, D, T])
    xbT = dram("xbT", [S, D, T])
    rcb = dram("rcb", [4, 64, T])
    rcf = dram("rcf", [4, 128, T])
    dbgst = dram("dbgst", [S, 32, 128, 8])
    dbgxh = dram("dbgxh", [S, D, T], BF16)
    dbghg = dram("dbghg", [S, NI2 * 128, T], BF16)
    dbgmu = dram("dbgmu", [S, 2, NCH, 512])

    sbt = lambda name, shape, dt=F32: es.enter_context(nc.sbuf_tensor(name, shape, dt))
    pst = lambda name, shape, dt=F32: es.enter_context(nc.psum_tensor(name, shape, dt))

    AR = sbt("AR", [128, 67584], BF16)
    SF = sbt("SF", [128, 7168], F32)
    SB = sbt("SB", [128, 16384], BF16)
    idf = sbt("idf", [128, 128], F32)
    idb = sbt("idb", [128, 64], BF16)
    idb128 = sbt("idb128", [128, 128], BF16)
    crn_s = sbt("crn_s", [128, 128], BF16)
    crn_e = sbt("crn_e", [128, 128], BF16)
    blkones = sbt("blkones", [128, 128], BF16)
    avgA = sbt("avgA", [128, 128], F32)
    avgB = sbt("avgB", [128, 128], F32)
    sel64 = sbt("sel64", [64, 128], F32)
    selM0 = sbt("selM0", [64, 128], F32)
    selM32 = sbt("selM32", [64, 128], F32)
    onescol = sbt("onescol", [128, 1], BF16)
    cst = sbt("cst", [128, 8], F32)
    vec_bproj = sbt("vec_bproj", [128, 8], F32)
    vec_lng = sbt("vec_lng", [128, 16], F32)
    vec_lnb = sbt("vec_lnb", [128, 16], F32)
    vec_b1 = sbt("vec_b1", [128, 2 * NF1], F32)
    vec_b2 = sbt("vec_b2", [128, 16], F32)
    vec_qs = sbt("vec_qs", [64, 8], F32)
    MUV = sbt("MUV", [64, 512], F32)
    RAB = sbt("RAB", [64, 512], F32)
    MS = sbt("MS", [128, 64], F32)
    CN = sbt("CN", [128, 128], F32)
    sbc = sbt("sbc_", [128, 512], F32)

    CST_EPS, CST_ONE, CST_LNB, CST_G38 = 0, 1, 2, 3

    duoq_t = pst("duoq", [128, 2048], F32)
    duoq = duoq_t[:, 0:2048]
    duo0 = duoq_t[:, 0:1024]
    duo1 = duoq_t[:, 1024:2048]
    pA = pst("pA", [128, 512], F32)
    pB = pst("pB", [128, 512], F32)
    pC = pst("pC", [128, 512], F32)
    ptr = pst("ptr", [128, 1024], BF16)

    DUO = [duo0[:, 0:512], duo0[:, 512:1024], duo1[:, 0:512], duo1[:, 512:1024]]

    B = Bld(nc, es)

    AR_W1 = 0
    AR_W2 = 45056
    AR_K = 0
    AR_V = 32768
    AR_WPJ = 66048
    AR_WQD = 61696
    SB_QB = 0
    SB_PB = 8192
    SB_OST = 10240
    AR_MSK = 0
    AR_WO = 4096
    AR_WQL = 8192
    AR_KW = 20480
    AR_VW = 28672
    AR_QW = 37376
    AR_PBL = 41472
    AR_OWL = 43520

    def arv(off, n, p0=0, p1=128):
        return AR[p0:p1, off:off + n]

    def vop(fn, waits=(), inc=True):
        return B.emit("vector", fn, waits=waits, inc=inc)

    def aop(fn, waits=(), inc=True):
        return B.emit("scalar", fn, waits=waits, inc=inc)

    def top(fn, waits=(), inc=True):
        return B.emit("tensor", fn, waits=waits, inc=inc)

    def gop(fn, waits=(), inc=True):
        return B.emit("gpsimd", fn, waits=waits, inc=inc)

    def ph0():
        gop(lambda e: e.memset(cst[:, CST_EPS:CST_EPS + 1], LN_EPS))
        gop(lambda e: e.memset(cst[:, CST_ONE:CST_ONE + 1], 1.0))
        gop(lambda e: e.memset(cst[:, CST_LNB:CST_LNB + 1], 25.6 / 89.6))
        gop(lambda e: e.memset(cst[:, CST_G38:CST_G38 + 1], 1e-38))
        gop(lambda e: e.memset(onescol[:], 1.0))
        gop(lambda e: e.memset(RAB[:], 1.0))
        gop(lambda e: e.memset(MUV[:], 0.0))
        gop(lambda e: e.memset(idf[:], 1.0))
        gop(lambda e: e.affine_select(out=idf[:], in_=idf[:], pattern=[[-1, 128]],
                                      compare_op=ALU.is_equal, fill=0.0,
                                      base=0, channel_multiplier=1))
        gop(lambda e: e.memset(idb[0:64, :], 1.0))
        midb = gop(lambda e: e.affine_select(out=idb[0:64, :], in_=idb[0:64, :],
                                             pattern=[[-1, 64]],
                                             compare_op=ALU.is_equal, fill=0.0,
                                             base=0, channel_multiplier=1))
        gop(lambda e: e.tensor_copy(out=idb[64:128, :], in_=idb[0:64, :]))
        gop(lambda e: e.memset(idb128[:], 1.0))
        gop(lambda e: e.affine_select(out=idb128[:], in_=idb128[:], pattern=[[-1, 128]],
                                      compare_op=ALU.is_equal, fill=0.0,
                                      base=0, channel_multiplier=1))
        gop(lambda e: e.memset(crn_s[:], 0.0))
        gop(lambda e: e.affine_select(out=crn_s[:], in_=crn_s[:], pattern=[[1, 128]],
                                      compare_op=ALU.is_ge, fill=NEG,
                                      base=0, channel_multiplier=-1))
        gop(lambda e: e.memset(crn_e[:], 0.0))
        gop(lambda e: e.affine_select(out=crn_e[:], in_=crn_e[:], pattern=[[-1, 128]],
                                      compare_op=ALU.is_ge, fill=NEG,
                                      base=0, channel_multiplier=1))
        gop(lambda e: e.memset(blkones[:], 0.0))
        gop(lambda e: e.memset(blkones[0:64, 0:64], 1.0))
        gop(lambda e: e.memset(blkones[64:128, 64:128], 1.0))
        gop(lambda e: e.memset(avgA[:], 0.0))
        gop(lambda e: e.memset(avgA[0:64, :], 1.0 / 64))
        gop(lambda e: e.memset(avgB[:], 0.0))
        gop(lambda e: e.memset(avgB[64:128, :], 1.0 / 64))
        gop(lambda e: e.memset(sel64[:], 0.0))
        gop(lambda e: e.memset(sel64[0:1, 0:64], 1.0))
        gop(lambda e: e.memset(sel64[32:33, 64:128], 1.0))
        gop(lambda e: e.memset(selM0[:], 0.0))
        gop(lambda e: e.memset(selM0[0:1, :], 1.0))
        gop(lambda e: e.memset(selM32[:], 0.0))
        mco = gop(lambda e: e.memset(selM32[32:33, :], 1.0))

        B.dma("sync", vec_bproj[:], bprojT)
        for l in range(2):
            B.dma("sync", vec_lng[:, l * 8:(l + 1) * 8], lngT[l])
            B.dma("sync", vec_lnb[:, l * 8:(l + 1) * 8], lnbT[l])
            B.dma("sync", vec_b1[:, l * NF1:(l + 1) * NF1], b1pT[l])
            B.dma("sync", vec_b2[:, l * 8:(l + 1) * 8], b2T[l])
            B.dma("sync", vec_qs[:, l * 4 + 0:l * 4 + 1], qsT[l])
            B.dma("sync", vec_qs[:, l * 4 + 1:l * 4 + 2], ksT[l])
            B.dma("sync", vec_qs[:, l * 4 + 2:l * 4 + 3], qswT[l])
            B.dma("sync", vec_qs[:, l * 4 + 3:l * 4 + 4], kswT[l])

        B.dma("gpsimd", arv(AR_WPJ, 1024).rearrange("p (g o) -> p g o", o=128),
              wproj.rearrange("g p o -> p g o"))
        B.dma("gpsimd", arv(AR_WQD, 3072).rearrange("p (g o) -> p g o", o=384),
              wqkv_d.rearrange("g p o -> p g o"))
        B.dma("gpsimd", arv(AR_WQL, 12288).rearrange("p (i o) -> p i o", o=1536),
              wqkv_l[0].rearrange("(i p) o -> p i o", p=128))

        tramp = SF[0:1, 0:4096]
        powr = SF[64:65, 0:4096]
        hix = SF[0:1, 4096:4160]
        ivf = SF[0:1, 4160:4224]
        lnb_ = SF[64:65, 4224:4288]
        m_t = gop(lambda e: e.iota(tramp, [[1, 4096]], base=0, channel_multiplier=0,
                                allow_small_or_imprecise_dtypes=True))
        m_p = gop(lambda e: e.iota(powr, [[1, 4096]], base=-2048, channel_multiplier=0,
                                allow_small_or_imprecise_dtypes=True))
        m_h = gop(lambda e: e.iota(hix, [[0, 2], [2, 32]], base=0, channel_multiplier=0,
                                allow_small_or_imprecise_dtypes=True))
        mA1 = aop(lambda e: e.activation(out=ivf, in_=hix, func=AF.Exp,
                                         scale=-LN1E4 / 64.0), waits=[m_h])
        mA2 = aop(lambda e: e.activation(out=lnb_, in_=hix, func=AF.Ln,
                                         bias=cst[0:1, CST_LNB:CST_LNB + 1],
                                         scale=1.0 / 89.6), waits=[m_h, mco])
        mpw = vop(lambda e: e.tensor_scalar(out=powr, in0=powr, scalar1=1.0 / 256,
                                            scalar2=None, op0=ALU.mult), waits=[m_p])

        G_, K_, R_, S_, C_ = 4608, 5120, 5632, 6144, 6656
        prev = []
        pab_guard = []
        for j in range(8):
            c0 = j * 512
            mf = top(lambda e, c0=c0: e.matmul(pA[0:64, :], ivf,
                                               SF[0:1, c0:c0 + 512],
                                               start=True, stop=True),
                     waits=[mA1, m_t] + pab_guard)
            ms = top(lambda e, c0=c0: e.matmul(pB[0:64, :], lnb_,
                                               SF[64:65, c0:c0 + 512],
                                               start=True, stop=True),
                     waits=[mA2, mpw])
            g_ = SF[0:64, G_:G_ + 512]
            k_ = SF[0:64, K_:K_ + 512]
            r_ = SF[0:64, R_:R_ + 512]
            s_ = SF[0:64, S_:S_ + 512]
            c_ = SF[0:64, C_:C_ + 512]
            m1 = vop(lambda e, g_=g_: e.tensor_scalar(out=g_, in0=pA[0:64, :],
                                                      scalar1=1.0 / TWO_PI, scalar2=None,
                                                      op0=ALU.mult),
                     waits=[mf] + prev)
            vop(lambda e, g_=g_, k_=k_: e.tensor_scalar(out=k_, in0=g_, scalar1=MAGIC,
                                                        scalar2=-MAGIC, op0=ALU.add,
                                                        op1=ALU.add))
            vop(lambda e, g_=g_, k_=k_, r_=r_: e.tensor_tensor(out=r_, in0=g_, in1=k_,
                                                               op=ALU.subtract))
            m4 = vop(lambda e, r_=r_: e.tensor_scalar(out=r_, in0=r_, scalar1=TWO_PI,
                                                      scalar2=None, op0=ALU.mult))
            msin = aop(lambda e, r_=r_, s_=s_: e.activation(out=s_, in_=r_, func=AF.Sin),
                       waits=[m4] + prev)
            aop(lambda e, r_=r_, c_=c_: e.activation(out=c_, in_=r_, func=AF.Sin,
                                                     scale=0.5))
            mh2 = aop(lambda e, c_=c_: e.activation(out=c_, in_=c_, func=AF.Square))
            mcos = vop(lambda e, c_=c_: e.tensor_scalar(out=c_, in0=c_, scalar1=-2.0,
                                                        scalar2=1.0, op0=ALU.mult,
                                                        op1=ALU.add), waits=[mh2])
            sg = SF[0:64, G_:G_ + 512]
            si = SF[0:64, K_:K_ + 512]
            msg = aop(lambda e, sg=sg: e.activation(out=sg, in_=pB[0:64, :],
                                                    func=AF.Exp), waits=[ms] + prev)
            msi = aop(lambda e, si=si: e.activation(out=si, in_=pB[0:64, :],
                                                    func=AF.Exp, scale=-1.0))
            pab_guard = [msi, m1]
            cs = SF[0:64, R_:R_ + 512]
            ssx = SF[0:64, G_:G_ + 512]
            ci = SF[0:64, C_:C_ + 512]
            sii = SF[0:64, S_:S_ + 512]
            w1_ = vop(lambda e, c_=c_, sg=sg, cs=cs: e.tensor_tensor(
                out=cs, in0=c_, in1=sg, op=ALU.mult), waits=[mcos, msg])
            w2_ = vop(lambda e, s_=s_, sg=sg, ssx=ssx: e.tensor_tensor(
                out=ssx, in0=s_, in1=sg, op=ALU.mult), waits=[msin])
            w3_ = vop(lambda e, c_=c_, si=si, ci=ci: e.tensor_tensor(
                out=ci, in0=c_, in1=si, op=ALU.mult), waits=[msi])
            w4_ = vop(lambda e, s_=s_, si=si, sii=sii: e.tensor_tensor(
                out=sii, in0=s_, in1=si, op=ALU.mult))
            f1 = vop(lambda e: e.tensor_scalar(out=SF[0:32, G_:G_ + 512],
                                               in0=SF[0:32, G_:G_ + 512],
                                               scalar1=-1.0, scalar2=None, op0=ALU.mult))
            f2 = vop(lambda e: e.tensor_scalar(out=SF[0:32, S_:S_ + 512],
                                               in0=SF[0:32, S_:S_ + 512],
                                               scalar1=-1.0, scalar2=None, op0=ALU.mult))
            prev = [
                B.dma("sync", rcb[0][0:64, c0:c0 + 512], cs, waits=[w1_], sem="dA"),
                B.dma("sync", rcb[1][0:64, c0:c0 + 512], ssx, waits=[f1], sem="dA"),
                B.dma("sync", rcb[2][0:64, c0:c0 + 512], ci, waits=[w3_], sem="dA"),
                B.dma("sync", rcb[3][0:64, c0:c0 + 512], sii, waits=[f2], sem="dA"),
            ]
        B.barrier()

    def fold_consts(l):
        spec = [(0, 0, 8.0), (1, 2, 8.0), (2, 1, 1.0), (3, 3, 1.0)]
        guards = [[] for _ in range(4)]
        i = 0
        for (k, col, mul) in spec:
            vc = vec_qs[:, l * 4 + col:l * 4 + col + 1]
            for j in range(8):
                c0 = j * 512
                sl = i % 4
                tile = SF[0:64, 4608 + sl * 512:4608 + (sl + 1) * 512]
                ml = B.dma("sync", tile, rcb[k][0:64, c0:c0 + 512],
                           waits=guards[sl], sem="dA" if sl % 2 == 0 else "dB")
                mm = vop(lambda e, vc=vc, mul=mul, tile=tile: e.tensor_scalar(
                    out=tile, in0=tile, scalar1=vc, scalar2=mul,
                    op0=ALU.mult, op1=ALU.mult), waits=[ml])
                guards[sl] = [
                    B.dma("sync", rcf[k][0:64, c0:c0 + 512], tile, waits=[mm],
                          sem="dC" if sl % 2 == 0 else "dD"),
                    B.dma("sync", rcf[k][64:128, c0:c0 + 512], tile, waits=[mm],
                          sem="dC" if sl % 2 == 0 else "dD"),
                ]
                i += 1
        B.barrier()

    def build_masks():
        for kc in range(8):
            mt = arv(AR_MSK + kc * 512, 512)
            gop(lambda e, mt=mt: e.memset(mt, 0.0))
            gop(lambda e, mt=mt, kc=kc: e.affine_select(
                out=mt, in_=mt, pattern=[[-1, 512]], compare_op=ALU.is_ge,
                fill=NEG, base=128 * kc, channel_multiplier=1))
            gop(lambda e, mt=mt, kc=kc: e.affine_select(
                out=mt, in_=mt, pattern=[[1, 512]], compare_op=ALU.is_ge,
                fill=NEG, base=512 - 128 * kc, channel_multiplier=-1))
        B.barrier()

    def s1a(s):
        X0A, X0B, XN, ST, EST, SQS = 0, 1024, 2048, 3072, 4096, 5120
        ESTB = 6144
        x0_guard = [[], []]
        bank_guard = [None, None]
        est_guard = [None, None]
        xn_guard = None
        m3_prev = None
        for g in range(32):
            sl = g % 2
            base = X0A if sl == 0 else X0B
            x0t = SF[:, base:base + 1024]
            ml = B.dma("sync", x0t, x0[s, g * 128:(g + 1) * 128, :],
                       waits=x0_guard[sl], sem="dA" if sl == 0 else "dB")
            msum = vop(lambda e, x0t=x0t: e.tensor_reduce(
                out=SF[:, ST:ST + 1], in_=x0t, axis=AXX, op=ALU.add), waits=[ml])
            msq = aop(lambda e, x0t=x0t: e.activation(
                out=SF[:, SQS:SQS + 1024], in_=x0t, func=AF.Square,
                accum_out=SF[:, ST + 1:ST + 2]), waits=[ml, m3_prev])
            m1 = vop(lambda e: e.tensor_scalar(out=SF[:, ST + 2:ST + 3],
                                               in0=SF[:, ST:ST + 1],
                                               scalar1=1.0 / D, scalar2=None,
                                               op0=ALU.mult), waits=[msum])
            vop(lambda e: e.tensor_tensor(out=SF[:, ST + 3:ST + 4],
                                          in0=SF[:, ST + 2:ST + 3],
                                          in1=SF[:, ST + 2:ST + 3], op=ALU.mult))
            m3 = vop(lambda e: e.tensor_scalar(out=SF[:, ST + 4:ST + 5],
                                               in0=SF[:, ST + 1:ST + 2],
                                               scalar1=1.0 / D, scalar2=None,
                                               op0=ALU.mult), waits=[msq])
            m3_prev = m3
            m4 = vop(lambda e: e.tensor_tensor(out=SF[:, ST + 4:ST + 5],
                                               in0=SF[:, ST + 4:ST + 5],
                                               in1=SF[:, ST + 3:ST + 4],
                                               op=ALU.subtract))
            aop(lambda e: e.activation(out=SF[:, ST + 5:ST + 6], in_=SF[:, ST + 4:ST + 5],
                                       func=AF.Ln, bias=cst[:, CST_EPS:CST_EPS + 1]),
                waits=[m4])
            m6 = aop(lambda e: e.activation(out=SF[:, ST + 6:ST + 7],
                                            in_=SF[:, ST + 5:ST + 6],
                                            func=AF.Exp, scale=-0.5))
            xnt = SF[:, XN:XN + 1024]
            m7 = vop(lambda e, x0t=x0t, xnt=xnt: e.tensor_scalar(
                out=xnt, in0=x0t, scalar1=SF[:, ST + 2:ST + 3],
                scalar2=SF[:, ST + 6:ST + 7], op0=ALU.subtract, op1=ALU.mult),
                waits=[m6, xn_guard])
            x0_guard[sl] = [m7, msq]
            if "dbgst" in dbg:
                B.dma("sync", dbgst[s][g], SF[:, ST:ST + 8], waits=[m7])
            eb = EST if g % 2 == 0 else ESTB
            mc = None
            for f in range(8):
                bk = pA if f % 2 == 0 else pB
                mt = top(lambda e, xnt=xnt, f=f, bk=bk: e.transpose(
                    bk[:, 0:128], xnt[:, f * 128:(f + 1) * 128], idf[:]),
                    waits=[m7, bank_guard[f % 2]])
                if f == 7:
                    xn_guard = mt
                mc = vop(lambda e, f=f, bk=bk, eb=eb: e.tensor_copy(
                    out=SF[:, eb + f * 128:eb + (f + 1) * 128], in_=bk[:, 0:128]),
                    waits=[mt, est_guard[g % 2] if f == 0 else None])
                bank_guard[f % 2] = mc
            est_guard[g % 2] = B.dma(
                "gpsimd",
                xnT[s][:, g * 128:(g + 1) * 128].rearrange("(f p) c -> p f c", p=128),
                SF[:, eb:eb + 1024].rearrange("p (f c) -> p f c", c=128),
                waits=[mc], sem="gC" if g % 2 == 0 else "gD")
        B.barrier()

    def s1b_dense(s):
        XB0, QST, KST, VT0, VST = 0, 4096, 4608, 5120, 13312
        XBA = 0
        QKA = 20480
        slot_guard = [None] * 4
        xb_guard = [[None] * 8, [None] * 8]
        stg_dma = {(0, 0): None, (0, 1): None, (1, 0): None, (1, 1): None}
        vt_guard = [None] * 8
        vst_dma = [None, None]
        ptr_guard = None
        for c in range(NCH):
            c0 = c * 512
            xb = XBA + (c % 2) * 4096
            mx = []
            for f in range(8):
                mx.append(B.dma("gpsimd", arv(xb + f * 512, 512),
                                xnT[s][f * 128:(f + 1) * 128, c0:c0 + 512],
                                waits=[xb_guard[c % 2][f]],
                                sem="gA" if c % 2 == 0 else "gB"))
            half = {}
            vt_cp = {}
            for g in range(8):
                for m in range(3):
                    idx = g * 3 + m
                    slot = DUO[idx % 4]
                    mqk = top(lambda e, g=g, m=m, slot=slot, xb=xb: e.matmul(
                        slot, arv(AR_WQD + g * 384 + m * 128, 128),
                        arv(xb + g * 512, 512),
                        start=True, stop=True),
                        waits=[mx[-1], slot_guard[idx % 4]])
                    xb_guard[c % 2][g] = mqk
                    last_cp = None
                    kind = (idx * 128) // 1024
                    if kind == 2:
                        for h2 in range(2):
                            fi = (idx * 128 + h2 * 64) % 1024
                            hh = fi // 64
                            src = slot[h2 * 64:(h2 + 1) * 64, :]
                            dst = SB[0:64, VT0 + hh * 512:VT0 + (hh + 1) * 512]
                            mcp = vop(lambda e, dst=dst, src=src: e.tensor_copy(
                                out=dst, in_=src), waits=[mqk, vt_guard[hh % 8]])
                            vt_cp[hh] = mcp
                            last_cp = mcp
                    else:
                        pr = ((idx * 128) % 1024) // 128
                        par = pr % 2
                        st = QKA + kind * 2048 + par * 512
                        mcp = vop(lambda e, st=st, slot=slot: e.tensor_copy(
                            out=arv(st, 512), in_=slot),
                            waits=[mqk, stg_dma[(kind, par)]])
                        tgt = (qdT if kind == 0 else kdT)[s][
                            pr * 128:(pr + 1) * 128, c0:c0 + 512]
                        stg_dma[(kind, par)] = B.dma(
                            "sync", tgt, arv(st, 512), waits=[mcp],
                            sem="dC" if kind == 0 else "dD")
                        last_cp = mcp
                    slot_guard[idx % 4] = last_cp
            for grp in range(4):
                vs = VST + (grp % 2) * 1040
                mms = gop(lambda e, vs=vs: e.memset(
                    SB[:, vs:vs + 1040].rearrange("p (h o) -> p h o", o=65)[:, :, 64:65],
                    1.0), waits=[vst_dma[grp % 2]])
                trs = None
                for h in range(16):
                    trs = top(lambda e, grp=grp, h=h: e.transpose(
                        ptr[:, h * 64:(h + 1) * 64],
                        SB[0:64, VT0 + h * 512 + grp * 128:VT0 + h * 512 + (grp + 1) * 128],
                        idb[0:64, :]),
                        waits=[vt_cp[h], ptr_guard])
                mcv = vop(lambda e, vs=vs: e.tensor_copy(
                    out=SB[:, vs:vs + 1040].rearrange("p (h o) -> p h o", o=65)[:, :, 0:64],
                    in_=ptr[:].rearrange("p (h o) -> p h o", o=64)),
                    waits=[trs, mms])
                ptr_guard = mcv
                vst_dma[grp % 2] = B.dma(
                    "sync", vdt[s][c0 + grp * 128:c0 + (grp + 1) * 128, :],
                    SB[:, vs:vs + 1040], waits=[mcv],
                    sem="dE" if grp % 2 == 0 else "dF")
            for pr in range(8):
                vt_guard[pr] = ptr_guard
        B.barrier()

    def s1b_local(s, l, src):
        XB0, VT0, VST, SQB, UHB = 0, 4096, 8192, 9232, 10256
        RC0, U_, M1_, M2_, RN_, RN2 = 0, 2048, 2560, 3072, 3584, 4096
        slot_guard = [None] * 4
        xb_guard = [None] * 8
        rc_guard = []
        sq_guard = [None, None]
        uh_guard = [None, None]
        rn2_guard = None
        vt_guard = [None] * 4
        vst_dma = [None, None]
        ptr_guard = None
        pc_guard = None
        ms_marks = []
        for c in range(NCH):
            c0 = c * 512
            mx = []
            for f in range(8):
                dst = SB[:, XB0 + f * 512:XB0 + (f + 1) * 512]
                gsem = "gA" if c % 2 == 0 else "gB"
                if f >= 4 and c == 0:
                    mz = gop(lambda e, dst=dst: e.memset(dst[:, 0:1], 0.0),
                             waits=[xb_guard[f]])
                    mx.append(B.dma("gpsimd", dst[:, 1:512],
                                    src[f * 128:(f + 1) * 128, 0:511], waits=[mz],
                                    sem=gsem))
                elif f >= 4:
                    mx.append(B.dma("gpsimd", dst,
                                    src[f * 128:(f + 1) * 128, c0 - 1:c0 + 511],
                                    waits=[xb_guard[f]], sem=gsem))
                else:
                    mx.append(B.dma("gpsimd", dst,
                                    src[f * 128:(f + 1) * 128, c0:c0 + 512],
                                    waits=[xb_guard[f]], sem=gsem))
            mrc = []
            for k in range(4):
                mrc.append(B.dma("sync", SF[:, RC0 + k * 512:RC0 + (k + 1) * 512],
                                 rcf[k][:, c0:c0 + 512], waits=rc_guard, sem="dA"))
            rc_guard = []
            sidx = 0
            pend = [None]

            def emit_tail():
                nonlocal rn2_guard, pc_guard
                if pend[0] is None:
                    return
                (si, slot, msq, kind_, pr_, sqb_) = pend[0]
                pend[0] = None
                mss = top(lambda e, sqb_=sqb_: e.matmul(
                    pC[:, 0:512], blkones[:], sqb_, start=True, stop=True),
                    waits=[msq, pc_guard])
                sq_guard[si % 2] = mss
                mln = aop(lambda e: e.activation(out=SF[:, RN_:RN_ + 512],
                                                 in_=pC[:, 0:512], func=AF.Ln),
                          waits=[mss])
                pc_guard = mln
                mrn = aop(lambda e: e.activation(out=SF[:, RN2:RN2 + 512],
                                                 in_=SF[:, RN_:RN_ + 512],
                                                 func=AF.Exp, scale=-0.5),
                          waits=[rn2_guard])
                mu = vop(lambda e, slot=slot: e.tensor_tensor(
                    out=SF[:, U_:U_ + 512], in0=slot, in1=SF[:, RN2:RN2 + 512],
                    op=ALU.mult), waits=[mrn])
                rn2_guard = mu
                slot_guard[si % 4] = mu
                A0 = RC0 + (0 if kind_ == 0 else 2) * 512
                B0 = RC0 + (1 if kind_ == 0 else 3) * 512
                mm1 = vop(lambda e, A0=A0: e.tensor_tensor(
                    out=SF[:, M1_:M1_ + 512], in0=SF[:, U_:U_ + 512],
                    in1=SF[:, A0:A0 + 512], op=ALU.mult), waits=mrc)
                for qtr in range(4):
                    b0 = qtr * 32
                    sq_ = 32 * ((qtr // 2) * 2 + (1 - qtr % 2))
                    vop(lambda e, b0=b0, sq_=sq_: e.tensor_copy(
                        out=SF[b0:b0 + 32, M2_:M2_ + 512],
                        in_=SF[sq_:sq_ + 32, U_:U_ + 512]))
                mq4 = vop(lambda e, B0=B0: e.tensor_tensor(
                    out=SF[:, M2_:M2_ + 512], in0=SF[:, M2_:M2_ + 512],
                    in1=SF[:, B0:B0 + 512], op=ALU.mult))
                rc_guard.extend([mm1, mq4])
                uh = SB[:, UHB + (si % 2) * 512:UHB + (si % 2) * 512 + 512]
                muh = vop(lambda e, uh=uh: e.tensor_tensor(
                    out=uh, in0=SF[:, M1_:M1_ + 512], in1=SF[:, M2_:M2_ + 512],
                    op=ALU.add), waits=[uh_guard[si % 2]])
                tgt = (qlT if kind_ == 0 else klT)[s][pr_ * 128:(pr_ + 1) * 128,
                                                      c0:c0 + 512]
                mdq = B.dma("sync", tgt, uh, waits=[muh],
                            sem="dC" if si % 2 == 0 else "dD")
                uh_guard[si % 2] = [mdq]

            for pr in range(4):
                for kind in range(2):
                    fo = kind * 4 + pr
                    slot = DUO[sidx % 4]
                    mons = None
                    for fi in range(8):
                        w = ([slot_guard[sidx % 4]] + mx) if fi == 0 else []
                        mons = top(lambda e, fi=fi, fo=fo, slot=slot: e.matmul(
                            slot, arv(AR_WQL + fi * 1536 + fo * 128, 128),
                            SB[:, XB0 + fi * 512:XB0 + (fi + 1) * 512],
                            start=(fi == 0), stop=(fi == 7)), waits=w)
                    for f in range(8):
                        xb_guard[f] = mons
                    sqb = SB[:, SQB + (sidx % 2) * 512:SQB + (sidx % 2) * 512 + 512]
                    msq = aop(lambda e, slot=slot, sqb=sqb: e.activation(
                        out=sqb, in_=slot, func=AF.Square),
                        waits=[mons, sq_guard[sidx % 2]])
                    emit_tail()
                    pend[0] = (sidx, slot, msq, kind, pr, sqb)
                    sidx += 1
            emit_tail()
            vt_cp = {}
            for pr in range(4):
                fo = 8 + pr
                slot = DUO[pr % 4]
                mons = None
                for fi in range(8):
                    w = ([slot_guard[pr % 4]] + mx) if fi == 0 else []
                    mons = top(lambda e, fi=fi, fo=fo, slot=slot: e.matmul(
                        slot, arv(AR_WQL + fi * 1536 + fo * 128, 128),
                        SB[:, XB0 + fi * 512:XB0 + (fi + 1) * 512],
                        start=(fi == 0), stop=(fi == 7)), waits=w)
                for f in range(8):
                    xb_guard[f] = mons
                mva = vop(lambda e, pr=pr, slot=slot: e.tensor_copy(
                    out=SB[0:64, VT0 + (2 * pr) * 512:VT0 + (2 * pr) * 512 + 512],
                    in_=slot[0:64, :]),
                    waits=[mons, vt_guard[pr]])
                mv = vop(lambda e, pr=pr, slot=slot: e.tensor_copy(
                    out=SB[0:64, VT0 + (2 * pr + 1) * 512:VT0 + (2 * pr + 1) * 512 + 512],
                    in_=slot[64:128, :]))
                slot_guard[pr % 4] = mv
                vt_cp[2 * pr] = mva
                vt_cp[2 * pr + 1] = mv
            for grp in range(4):
                vs = VST + (grp % 2) * 520
                mms = gop(lambda e, vs=vs: e.memset(
                    SB[:, vs:vs + 520].rearrange("p (h o) -> p h o", o=65)[:, :, 64:65],
                    1.0), waits=[vst_dma[grp % 2]])
                trs = None
                for h in range(8):
                    trs = top(lambda e, grp=grp, h=h: e.transpose(
                        ptr[:, h * 64:(h + 1) * 64],
                        SB[0:64, VT0 + h * 512 + grp * 128:VT0 + h * 512 + (grp + 1) * 128],
                        idb[0:64, :]),
                        waits=[vt_cp[h], ptr_guard])
                mcv = vop(lambda e, vs=vs: e.tensor_copy(
                    out=SB[:, vs:vs + 520].rearrange("p (h o) -> p h o", o=65)[:, :, 0:64],
                    in_=ptr[:, 0:512].rearrange("p (h o) -> p h o", o=64)),
                    waits=[trs, mms])
                ptr_guard = mcv
                vst_dma[grp % 2] = B.dma(
                    "sync", vlt[s][c0 + grp * 128:c0 + (grp + 1) * 128, :],
                    SB[:, vs:vs + 520], waits=[mcv],
                    sem="dE" if grp % 2 == 0 else "dF")
            for pr in range(4):
                vt_guard[pr] = ptr_guard
        B.barrier()

    def s2(s, tick=None):
        mk = []
        for pr in range(8):
            mk.append(B.dma("sync", arv(AR_K + pr * 4096, 4096),
                            kdT[s][pr * 128:(pr + 1) * 128, :], sem="dA"))
        for kc in range(32):
            mk.append(B.dma("sync", arv(AR_V + kc * 1040, 1040),
                            vdt[s][kc * 128:(kc + 1) * 128, :], sem="dA"))
        mk = mk[-1:]
        qb_guard = [None, None]
        pv_guard = [None, None]
        ost_dma = [None, None]
        pc_guard = None
        rab_guard = None
        prev_exp = [None, None]
        prev_pv = [None, None]
        pend_norm = None
        STG0 = 5120
        for pr in range(8):
            qb = SB_QB + (pr % 2) * 4096
            mq = B.dma("sync", SB[:, qb:qb + 4096], qdT[s][pr * 128:(pr + 1) * 128, :],
                       waits=[qb_guard[pr % 2]], sem="dB" if pr % 2 == 0 else "dC")
            last_mm = None
            for qc in range(8):
                q0 = qc * 512
                mexp = {}
                mpvk = {}

                def emit_pv(kc, pr=pr, q0=q0):
                    pb = SB_PB + (kc % 2) * 1024
                    wl3 = [mexp[kc]]
                    wlB = []
                    if kc == 0:
                        wl3.append(pv_guard[0])
                        wlB.append(pv_guard[1])
                    top(lambda e, kc=kc, pr=pr, pb=pb: e.matmul(
                        pA[0:65, :], arv(AR_V + kc * 1040 + (2 * pr) * 65, 65),
                        SB[:, pb:pb + 512], start=(kc == 0), stop=(kc == 31)),
                        waits=wl3)
                    mpv = top(lambda e, kc=kc, pr=pr, pb=pb: e.matmul(
                        pB[0:65, :], arv(AR_V + kc * 1040 + (2 * pr + 1) * 65, 65),
                        SB[:, pb + 512:pb + 1024], start=(kc == 0), stop=(kc == 31)),
                        waits=wlB)
                    mpvk[kc] = mpv
                    return mpv

                for kc in range(32):
                    if kc == 8 and pend_norm is not None:
                        pend_norm()
                        pend_norm = None
                    duo = duo0 if kc % 2 == 0 else duo1
                    wl = [mexp.get(kc - 2) if kc >= 2 else prev_exp[kc]]
                    if kc == 0:
                        wl += mk + [mq]
                    top(lambda e, pr=pr, kc=kc, qb=qb, q0=q0, duo=duo: e.matmul(
                        duo[:, 0:512],
                        arv(AR_K + pr * 4096 + kc * 128, 128, 0, 64),
                        SB[0:64, qb + q0:qb + q0 + 512],
                        start=True, stop=True, tile_position=(0, 0)), waits=wl)
                    mqk = top(lambda e, pr=pr, kc=kc, qb=qb, q0=q0, duo=duo: e.matmul(
                        duo[:, 512:1024],
                        arv(AR_K + pr * 4096 + kc * 128, 128, 64, 128),
                        SB[64:128, qb + q0:qb + q0 + 512],
                        start=True, stop=True, tile_position=(64, 0)))
                    pb = SB_PB + (kc % 2) * 1024
                    me = aop(lambda e, duo=duo, pb=pb: e.activation(
                        out=SB[:, pb:pb + 1024], in_=duo[:], func=AF.Exp, scale=0.125),
                        waits=[mqk, mpvk.get(kc - 2) if kc >= 2 else prev_pv[kc]])
                    mexp[kc] = me
                    if kc >= 1:
                        last_mm = emit_pv(kc - 1)
                last_mm = emit_pv(31)
                prev_exp = [mexp[30], mexp[31]]
                prev_pv = [mpvk[30], mpvk[31]]
                stg = STG0 + (qc % 2) * 512
                cA = vop(lambda e, stg=stg: e.tensor_copy(
                    out=SF[0:64, stg:stg + 512], in_=pA[0:64, :]),
                    waits=[last_mm])
                rA = vop(lambda e: e.tensor_copy(out=RAB[0:1, :], in_=pA[64:65, :]),
                         waits=[rab_guard])
                cB = vop(lambda e, stg=stg: e.tensor_copy(
                    out=SF[64:128, stg:stg + 512], in_=pB[0:64, :]))
                rB = vop(lambda e: e.tensor_copy(out=RAB[32:33, :], in_=pB[64:65, :]))
                pv_guard = [rA, rB]

                def mk_norm(qc=qc, stg=stg, q0=q0, pr=pr):
                    nonlocal pc_guard, rab_guard
                    rc = vop(lambda e: e.reciprocal(out=RAB[0:33, :], in_=RAB[0:33, :]))
                    mbc = top(lambda e: e.matmul(pC[:, 0:512], sel64[:], RAB[:],
                                                 start=True, stop=True),
                              waits=[rc, pc_guard])
                    rab_guard = mbc
                    ost = SB_OST + (qc % 2) * 512
                    o1 = vop(lambda e, ost=ost, stg=stg: e.tensor_tensor(
                        out=SB[0:64, ost:ost + 512], in0=SF[0:64, stg:stg + 512],
                        in1=pC[0:64, :], op=ALU.mult),
                        waits=[mbc, ost_dma[qc % 2]])
                    o2 = vop(lambda e, ost=ost, stg=stg: e.tensor_tensor(
                        out=SB[64:128, ost:ost + 512], in0=SF[64:128, stg:stg + 512],
                        in1=pC[64:128, :], op=ALU.mult))
                    pc_guard = o2
                    ost_dma[qc % 2] = B.dma(
                        "sync", odT[s][pr * 128:(pr + 1) * 128, q0:q0 + 512],
                        SB[:, ost:ost + 512], waits=[o1, o2],
                        sem="dD" if qc % 2 == 0 else "dE")
                pend_norm = mk_norm
            qb_guard[pr % 2] = last_mm
            if tick is not None:
                tick()
        if pend_norm is not None:
            pend_norm()
            pend_norm = None
        B.barrier()

    def s3_gen(s):
        OD0 = 12288
        pv3 = ptr[:].bitcast(F32)
        od_guard = [None] * 8
        slot_guard = None
        stg_dma = [None, None]
        pend3 = None
        for c in range(NCH):
            c0 = c * 512
            for h in range(2):
                mo = []
                for g4 in range(4):
                    g = h * 4 + g4
                    mo.append(B.dma("gpsimd",
                                    SB[:, OD0 + g * 512:OD0 + (g + 1) * 512],
                                    odT[s][g * 128:(g + 1) * 128, c0:c0 + 512],
                                    waits=[od_guard[g]], sem="dG"))
                cur = (mo[-1], c0, h)
                if pend3 is not None:
                    (mld, pc0, ph) = pend3
                    for g4 in range(4):
                        g = ph * 4 + g4
                        mm = top(lambda e, g=g: e.matmul(
                            pv3, arv(AR_WPJ + g * 128, 128),
                            SB[:, OD0 + g * 512:OD0 + (g + 1) * 512],
                            start=True, stop=True), waits=[mld, slot_guard])
                        od_guard[g] = mm
                        st = 4096 + (g % 2) * 512
                        me = vop(lambda e, st=st, g=g, pc0=pc0: e.tensor_scalar(
                            out=SF[:, st:st + 512], in0=pv3,
                            scalar1=vec_bproj[:, g:g + 1],
                            scalar2=None, op0=ALU.add), waits=[mm, stg_dma[g % 2]])
                        slot_guard = me
                        stg_dma[g % 2] = B.dma(
                            "gpsimd", dnT[s][g * 128:(g + 1) * 128, pc0:pc0 + 512],
                            SF[:, st:st + 512], waits=[me], sem="dH")
                pend3 = cur
            yield
        if pend3 is not None:
            (mld, pc0, ph) = pend3
            for g4 in range(4):
                g = ph * 4 + g4
                mm = top(lambda e, g=g: e.matmul(
                    pv3, arv(AR_WPJ + g * 128, 128),
                    SB[:, OD0 + g * 512:OD0 + (g + 1) * 512],
                    start=True, stop=True), waits=[mld, slot_guard])
                od_guard[g] = mm
                st = 4096 + (g % 2) * 512
                me = vop(lambda e, st=st, g=g, pc0=pc0: e.tensor_scalar(
                    out=SF[:, st:st + 512], in0=pv3,
                    scalar1=vec_bproj[:, g:g + 1],
                    scalar2=None, op0=ALU.add), waits=[mm, stg_dma[g % 2]])
                slot_guard = me
                stg_dma[g % 2] = B.dma(
                    "gpsimd", dnT[s][g * 128:(g + 1) * 128, pc0:pc0 + 512],
                    SF[:, st:st + 512], waits=[me], sem="dH")
        yield

    def s3(s):
        for _ in s3_gen(s):
            pass
        B.barrier()

    def s4(s, l):
        PST = AR_PBL
        pv_guard = [None, None]
        pc_guard = None
        rab_guard = None
        win_guard = [None, None]
        ptr_guard = None
        stage_guard = [None, None]
        mx_guard = [None] * 8
        last_exp = [None, None]
        pend_norm = None
        for w in range(NW):
            kws = AR_KW + (w % 2) * 4096
            vws = AR_VW + (w % 2) * 4160
            qws = AR_QW + (w % 2) * 2048
            mks = []
            wg = [win_guard[w % 2]]
            wsem = "dA" if w % 2 == 0 else "dB"
            for pr in range(4):
                if w == 0:
                    mks.append(B.dma("sync", arv(kws + pr * 1024 + 512, 512),
                                     klT[s][pr * 128:(pr + 1) * 128, 0:512], waits=wg,
                                     sem=wsem))
                else:
                    mks.append(B.dma("sync", arv(kws + pr * 1024, 1024),
                                     klT[s][pr * 128:(pr + 1) * 128,
                                            (w - 1) * 512:(w + 1) * 512], waits=wg,
                                     sem=wsem))
            kc_lo0 = 4 if w == 0 else 0
            for kc in range(kc_lo0, 8):
                t0 = (w - 1) * 512 + kc * 128
                mks.append(B.dma("sync", arv(vws + kc * 520, 520),
                                 vlt[s][t0:t0 + 128, :], waits=wg, sem=wsem))
            for pr in range(4):
                mks.append(B.dma("sync", arv(qws + pr * 512, 512),
                                 qlT[s][pr * 128:(pr + 1) * 128,
                                        w * 512:(w + 1) * 512], waits=wg, sem=wsem))
            mks = mks[-1:]
            last_mm = None
            pend = [None]

            def emit_back(vws=vws):
                nonlocal last_mm, ptr_guard
                if pend[0] is None:
                    return
                (qt_, c_lo_, nch_, exps, pbufs, first_pv) = pend[0]
                pend[0] = None
                for hb in range(2):
                    pv = pA if hb == 0 else pB
                    trs = None
                    for j in range(nch_):
                        trs = top(lambda e, j=j, pbuf=pbufs[hb]: e.transpose(
                            ptr[:, j * 128:(j + 1) * 128],
                            pbuf[:, j * 128:(j + 1) * 128], idb128[:]),
                            waits=[exps[hb], ptr_guard])
                    mx_guard[hb * 2 + (qt_ % 2)] = trs
                    mev = vop(lambda e, nk=nch_ * 128, hb=hb: e.tensor_copy(
                        out=arv(PST + hb * 640, 640)[:, 0:nk],
                        in_=ptr[:, 0:nk]), waits=[trs, stage_guard[hb]])
                    ptr_guard = mev
                    wpv = [mev]
                    if first_pv:
                        wpv.append(pv_guard[hb])
                    mpv = None
                    for j in range(nch_):
                        kc = c_lo_ + j
                        mpv = top(lambda e, j=j, kc=kc, hb=hb, qt=qt_, pv=pv,
                                  vws=vws, st=(qt_ == 3 and j == nch_ - 1),
                                  fp=(first_pv and j == 0), pr_=pend_pr[0]:
                                  e.matmul(
                            pv[0:65, qt * 128:(qt + 1) * 128],
                            arv(vws + kc * 520 + (2 * pr_ + hb) * 65, 65),
                            arv(PST + hb * 640 + j * 128, 128),
                            start=fp, stop=st),
                            waits=wpv if j == 0 else [])
                    stage_guard[hb] = mpv
                    last_mm = mpv

            pend_pr = [0]
            for pr in range(4):
                for qt in range(4):
                    ti = pr * 4 + qt
                    c_lo = max(kc_lo0, qt)
                    nch_ = qt + 5 - c_lo
                    w0 = c_lo * 128
                    nk = nch_ * 128
                    exps = [None, None]
                    pbufs = [None, None]
                    if qt == 2 and pend_norm is not None:
                        pend_norm()
                        pend_norm = None
                    for hb, duo in ((0, duo0), (1, duo1)):
                        prev_mex = last_exp[hb]
                        wl = [prev_mex]
                        if pr == 0 and qt == 0:
                            wl += mks
                        pieces = [(0, min(nk, 512))]
                        if nk > 512:
                            pieces.append((512, nk - 512))
                        first = True
                        mqk = None
                        for (o0, n_) in pieces:
                            mqk = top(lambda e, pr=pr, hb=hb, qt=qt, kws=kws,
                                      qws=qws, duo=duo, o0=o0, n_=n_, w0=w0:
                                      e.matmul(
                                duo[:, o0:o0 + n_],
                                AR[hb * 64:(hb + 1) * 64,
                                   qws + pr * 512 + qt * 128:qws + pr * 512 + (qt + 1) * 128],
                                AR[hb * 64:(hb + 1) * 64,
                                   kws + pr * 1024 + w0 + o0:kws + pr * 1024 + w0 + o0 + n_],
                                start=True, stop=False,
                                tile_position=(hb * 64, 0)), waits=wl if first else [])
                            first = False
                        if c_lo == qt:
                            top(lambda e, duo=duo: e.matmul(
                                duo[:, 0:128], idb128[:], crn_s[:],
                                start=False, stop=True))
                        mqk = top(lambda e, duo=duo, nk=nk: e.matmul(
                            duo[:, nk - 128:nk], idb128[:], crn_e[:],
                            start=False, stop=True))
                        mxc = MS[:, hb * 2 + ti % 2:hb * 2 + ti % 2 + 1]
                        mmx = vop(lambda e, duo=duo, nk=nk, mxc=mxc: e.tensor_reduce(
                            out=mxc, in_=duo[:, 0:nk], axis=AXX, op=ALU.max,
                            negate=True), waits=[mqk, prev_mex])
                        pbuf = arv(AR_OWL + 2048 + (hb * 2 + ti % 2) * 1024, 1024)
                        mex = aop(lambda e, duo=duo, nk=nk, pbuf=pbuf, mxc=mxc:
                                  e.activation(out=pbuf[:, 0:nk], in_=duo[:, 0:nk],
                                               func=AF.Exp, bias=mxc),
                                  waits=[mmx, mx_guard[hb * 2 + ti % 2]])
                        last_exp[hb] = mex
                        exps[hb] = mex
                        pbufs[hb] = pbuf
                    emit_back()
                    pend[0] = (qt, c_lo, nch_, exps, pbufs, qt == 0)
                    pend_pr[0] = pr
                emit_back()
                stg = 5120 + (pr % 2) * 512
                cA = vop(lambda e, stg=stg: e.tensor_copy(
                    out=SF[0:64, stg:stg + 512], in_=pA[0:64, :]),
                    waits=[last_mm])
                rA = vop(lambda e: e.tensor_scalar(
                    out=RAB[0:1, :], in0=pA[64:65, :],
                    scalar1=cst[0:1, CST_G38:CST_G38 + 1], scalar2=None, op0=ALU.add),
                    waits=[rab_guard])
                cB = vop(lambda e, stg=stg: e.tensor_copy(
                    out=SF[64:128, stg:stg + 512], in_=pB[0:64, :]))
                rB = vop(lambda e: e.tensor_scalar(
                    out=RAB[32:33, :], in0=pB[64:65, :],
                    scalar1=cst[32:33, CST_G38:CST_G38 + 1], scalar2=None, op0=ALU.add))
                pv_guard = [rA, rB]

                def mk_norm(pr=pr, w=w, stg=stg, rB=rB):
                    nonlocal pc_guard, rab_guard
                    rln = aop(lambda e: e.activation(
                        out=RAB[0:33, :], in_=RAB[0:33, :], func=AF.Ln),
                        waits=[rB])
                    rex = aop(lambda e: e.activation(
                        out=RAB[0:33, :], in_=RAB[0:33, :], func=AF.Exp, scale=-1.0))
                    mbc = top(lambda e: e.matmul(pC[:, 0:512], sel64[:], RAB[:],
                                                 start=True, stop=True),
                              waits=[rex, pc_guard])
                    rab_guard = mbc
                    o1 = vop(lambda e, pr=pr, stg=stg: e.tensor_tensor(
                        out=arv(AR_OWL + pr * 512, 512, 0, 64),
                        in0=SF[0:64, stg:stg + 512], in1=pC[0:64, :], op=ALU.mult),
                        waits=[mbc, B.now("dC")])
                    o2 = vop(lambda e, pr=pr, stg=stg: e.tensor_tensor(
                        out=arv(AR_OWL + pr * 512, 512, 64, 128),
                        in0=SF[64:128, stg:stg + 512], in1=pC[64:128, :], op=ALU.mult))
                    pc_guard = o2
                    B.dma("sync", olT[s][pr * 128:(pr + 1) * 128, w * 512:(w + 1) * 512],
                          arv(AR_OWL + pr * 512, 512), waits=[o1, o2], sem="dC")
                pend_norm = mk_norm
            win_guard[w % 2] = last_mm
        if pend_norm is not None:
            pend_norm()
            pend_norm = None
        B.barrier()

    def s5(s, l, srcT, dstT):
        O0, SRC, OUTS = 0, 4096, 5120
        o_guard = [None] * 4
        slot_guard = [None] * 4
        src_guard = [None, None]
        out_dma = [None, None]
        for c in range(NCH):
            c0 = c * 512
            mo = []
            for fi in range(4):
                mo.append(B.dma("sync", SB[:, O0 + fi * 512:O0 + (fi + 1) * 512],
                                olT[s][fi * 128:(fi + 1) * 128, c0:c0 + 512],
                                waits=[o_guard[fi]], sem="dA" if c % 2 == 0 else "dB"))
            for g in range(8):
                slot = DUO[g % 4]
                mons = None
                for fi in range(4):
                    w = (mo + [slot_guard[g % 4]]) if fi == 0 else []
                    mons = top(lambda e, g=g, fi=fi, slot=slot: e.matmul(
                        slot, SB[:, 2048 + fi * 1024 + g * 128:2048 + fi * 1024 + (g + 1) * 128],
                        SB[:, O0 + fi * 512:O0 + (fi + 1) * 512],
                        start=(fi == 0), stop=(fi == 3)), waits=w)
                for fi in range(4):
                    o_guard[fi] = mons
                st = SRC + (g % 2) * 512
                st2 = OUTS + (g % 2) * 512
                ms_ = B.dma("sync", SF[:, st:st + 512],
                            srcT[g * 128:(g + 1) * 128, c0:c0 + 512],
                            waits=[src_guard[g % 2]], sem="dE" if g % 2 == 0 else "dF")
                ma = vop(lambda e, slot=slot, st=st, st2=st2: e.tensor_tensor(
                    out=SF[:, st2:st2 + 512], in0=slot, in1=SF[:, st:st + 512],
                    op=ALU.add), waits=[mons, ms_, out_dma[g % 2]])
                slot_guard[g % 4] = ma
                src_guard[g % 2] = ma
                out_dma[g % 2] = B.dma(
                    "gpsimd", dstT[g * 128:(g + 1) * 128, c0:c0 + 512],
                    SF[:, st2:st2 + 512], waits=[ma],
                    sem="dC" if g % 2 == 0 else "dD")
        B.barrier()

    duo_guard = [None] * 4

    def s6(s, l, srcT, dstT, final):
        XH, SQ2, HA = 0, 4096, 5120
        MUB, RSB, GL = 4096, 4608, 5120
        xb_guard = [None] * 8
        sq_guard = [None, None]
        lnin_guard = [None, None, None]
        xh_guard = [None] * 8
        ha_guard = [None] * NI2
        res_guard = [None, None]
        dn_guard = [None, None]
        out_dma = [None, None]
        stat_guard = None
        fin_guard = [None] * 8
        x0_guard = [None, None]
        w2slot_guard = [None, None]
        hlast = None
        lnlast = None
        for c in range(NCH):
            c0 = c * 512
            mxb = []
            gsem = "gA" if c % 2 == 0 else "gB"
            if c == 0:
                for f in range(8):
                    dst = SB[:, XH + f * 512:XH + (f + 1) * 512]
                    if f >= 4:
                        mz = gop(lambda e, dst=dst: e.memset(dst[:, 0:1], 0.0),
                                 waits=[xb_guard[f]])
                        mxb.append(B.dma("gpsimd", dst[:, 1:512],
                                         srcT[f * 128:(f + 1) * 128, 0:511],
                                         waits=[mz], sem=gsem))
                    else:
                        mxb.append(B.dma("gpsimd", dst,
                                         srcT[f * 128:(f + 1) * 128, c0:c0 + 512],
                                         waits=[xb_guard[f]], sem=gsem))
            else:
                mxb.append(B.dma(
                    "gpsimd",
                    SB[:, XH:XH + 2048].rearrange("p (f t) -> p f t", t=512),
                    srcT[:, c0:c0 + 512].rearrange("(f p) t -> p f t", p=128)[:, 0:4],
                    waits=xb_guard[0:4], sem=gsem))
                mxb.append(B.dma(
                    "gpsimd",
                    SB[:, XH + 2048:XH + 4096].rearrange("p (f t) -> p f t", t=512),
                    srcT[:, c0 - 1:c0 + 511].rearrange("(f p) t -> p f t", p=128)[:, 4:8],
                    waits=xb_guard[4:8], sem=gsem))
            msum = None
            msumsq = None
            for f in range(8):
                sq = SB[:, SQ2 + (f % 2) * 512:SQ2 + (f % 2) * 512 + 512]
                msq = aop(lambda e, f=f, sq=sq: e.activation(
                    out=sq, in_=SB[:, XH + f * 512:XH + (f + 1) * 512],
                    func=AF.Square), waits=[mxb[-1], sq_guard[f % 2]])
                wsum = [mxb[-1]]
                if f == 0:
                    wsum += [stat_guard, w2slot_guard[0], w2slot_guard[1]]
                msum = top(lambda e, f=f: e.matmul(
                    pA[0:1, :], onescol[:], SB[:, XH + f * 512:XH + (f + 1) * 512],
                    start=(f == 0), stop=(f == 7)), waits=wsum)
                wsq = [msq]
                if f == 0:
                    wsq += [stat_guard, w2slot_guard[0], w2slot_guard[1]]
                msumsq = top(lambda e, f=f, sq=sq: e.matmul(
                    pB[0:1, :], onescol[:], sq, start=(f == 0), stop=(f == 7)),
                    waits=wsq)
                sq_guard[f % 2] = msumsq
            ms1 = vop(lambda e: e.tensor_scalar(out=MUV[0:1, :], in0=pA[0:1, :],
                                                scalar1=1.0 / D, scalar2=None,
                                                op0=ALU.mult), waits=[msum])
            vop(lambda e: e.tensor_tensor(out=RAB[32:33, :], in0=MUV[0:1, :],
                                          in1=MUV[0:1, :], op=ALU.mult))
            vop(lambda e: e.tensor_scalar(out=MUV[32:33, :], in0=pB[0:1, :],
                                          scalar1=1.0 / D, scalar2=None,
                                          op0=ALU.mult), waits=[msumsq])
            ms4 = vop(lambda e: e.tensor_tensor(out=MUV[32:33, :], in0=MUV[32:33, :],
                                                in1=RAB[32:33, :], op=ALU.subtract))
            aop(lambda e: e.activation(out=RAB[32:33, :], in_=MUV[32:33, :],
                                       func=AF.Ln, bias=cst[32:33, CST_EPS:CST_EPS + 1]),
                waits=[ms4])
            ms6 = aop(lambda e: e.activation(out=MUV[32:33, :], in_=RAB[32:33, :],
                                             func=AF.Exp, scale=-0.5))
            if "dbgmu" in dbg and l == 0:
                B.dma("sync", dbgmu[s][0, c], MUV[0:1, :], waits=[ms6], sem="dH")
                B.dma("sync", dbgmu[s][1, c], MUV[32:33, :], waits=[ms6], sem="dH")
            mb1 = top(lambda e: e.matmul(pA[:, 0:512], selM0[:], MUV[:],
                                         start=True, stop=True), waits=[ms6, ms1])
            mb2 = top(lambda e: e.matmul(pB[:, 0:512], selM32[:], MUV[:],
                                         start=True, stop=True))
            wme = [res_guard[0], res_guard[1], x0_guard[0], x0_guard[1]]
            me1 = vop(lambda e: e.tensor_copy(out=SF[:, MUB:MUB + 512],
                                              in_=pA[:, 0:512]), waits=[mb1] + wme)
            me2 = vop(lambda e: e.tensor_copy(out=SF[:, RSB:RSB + 512],
                                              in_=pB[:, 0:512]), waits=[mb2])
            stat_guard = me2
            for f in range(8):
                lt = SF[:, (f % 3) * 512:(f % 3) * 512 + 512]
                wlt = [lnin_guard[f % 3]]
                if f >= 4 and c == 0:
                    mzl = vop(lambda e, lt=lt: e.tensor_scalar(
                        out=lt[:, 0:1], in0=cst[:, CST_ONE:CST_ONE + 1],
                        scalar1=0.0, scalar2=None, op0=ALU.mult), waits=wlt)
                    mlt = B.dma("sync", lt[:, 1:512],
                                srcT[f * 128:(f + 1) * 128, 0:511], waits=[mzl],
                                sem="dA")
                elif f >= 4:
                    mlt = B.dma("sync", lt, srcT[f * 128:(f + 1) * 128, c0 - 1:c0 + 511],
                                waits=wlt, sem="dA")
                else:
                    mlt = B.dma("sync", lt, srcT[f * 128:(f + 1) * 128, c0:c0 + 512],
                                waits=wlt, sem="dA")
                vop(lambda e, lt=lt: e.tensor_tensor(
                    out=lt, in0=lt, in1=SF[:, MUB:MUB + 512], op=ALU.subtract),
                    waits=[mlt, me1])
                vop(lambda e, lt=lt: e.tensor_tensor(
                    out=lt, in0=lt, in1=SF[:, RSB:RSB + 512], op=ALU.mult),
                    waits=[me2])
                m3 = vop(lambda e, lt=lt, f=f: e.tensor_scalar(
                    out=SB[:, XH + f * 512:XH + (f + 1) * 512], in0=lt,
                    scalar1=vec_lng[:, l * 8 + f:l * 8 + f + 1],
                    scalar2=vec_lnb[:, l * 8 + f:l * 8 + f + 1],
                    op0=ALU.mult, op1=ALU.add),
                    waits=[msum, msumsq])
                lnin_guard[f % 3] = m3
                xh_guard[f] = m3
                lnlast = m3
                if "dbgxh" in dbg and l == 0:
                    xh_guard[f] = B.dma(
                        "sync", dbgxh[s][f * 128:(f + 1) * 128, c0:c0 + 512],
                        SB[:, XH + f * 512:XH + (f + 1) * 512], waits=[m3], sem="dG")
            for j in range(NI2):
                sa_ = DUO[(2 * j) % 4]
                sg_ = DUO[(2 * j + 1) % 4]
                ma = None
                for fi in range(8):
                    wa = [xh_guard[fi]]
                    if fi == 0:
                        wa.append(duo_guard[(2 * j) % 4])
                    ma = top(lambda e, fi=fi, j=j, sa_=sa_: e.matmul(
                        sa_, arv(AR_W1 + fi * 5632 + (2 * j) * 128, 128),
                        SB[:, XH + fi * 512:XH + (fi + 1) * 512],
                        start=(fi == 0), stop=(fi == 7)), waits=wa)
                mg = None
                for fi in range(8):
                    wg_ = [duo_guard[(2 * j + 1) % 4]] if fi == 0 else []
                    mg = top(lambda e, fi=fi, j=j, sg_=sg_: e.matmul(
                        sg_, arv(AR_W1 + fi * 5632 + (2 * j + 1) * 128, 128),
                        SB[:, XH + fi * 512:XH + (fi + 1) * 512],
                        start=(fi == 0), stop=(fi == 7)), waits=wg_)
                gl = SF[:, GL + (j % 2) * 512:GL + (j % 2) * 512 + 512]
                wgl = [mg, ha_guard_src[j % 2], dn_guard[j % 2], out_dma[j % 2]]
                mgl = aop(lambda e, gl=gl, sg_=sg_, j=j: e.activation(
                    out=gl, in_=sg_, func=AF.Gelu,
                    bias=vec_b1[:, l * NF1 + 2 * j + 1:l * NF1 + 2 * j + 2]),
                    waits=wgl)
                duo_guard[(2 * j + 1) % 4] = mgl
                mtmp = vop(lambda e, j=j, sa_=sa_: e.tensor_scalar(
                    out=SF[:, 6144:6656], in0=sa_,
                    scalar1=vec_b1[:, l * NF1 + 2 * j:l * NF1 + 2 * j + 1],
                    scalar2=None, op0=ALU.add), waits=[ma])
                mh = vop(lambda e, j=j, gl=gl: e.tensor_tensor(
                    out=SB[:, HA + j * 512:HA + (j + 1) * 512], in0=SF[:, 6144:6656],
                    in1=gl, op=ALU.mult),
                    waits=[mgl, ha_guard[j]])
                duo_guard[(2 * j) % 4] = mh
                ha_guard_src[j % 2] = mh
                hlast = mh
                if "dbghg" in dbg and l == 0:
                    m_ = B.dma(
                        "sync", dbghg[s][j * 128:(j + 1) * 128, c0:c0 + 512],
                        SB[:, HA + j * 512:HA + (j + 1) * 512], waits=[mh], sem="dG")
                    if "dbgsync" in dbg:
                        hlast = m_
                        ha_guard_src[j % 2] = m_
            for f in range(8):
                xb_guard[f] = mg
            for fo in range(8):
                bank = pA if fo % 2 == 0 else pB
                mw = None
                for fi in range(NI2):
                    ww = []
                    if fi == 0:
                        ww = [w2slot_guard[fo % 2], hlast, ms1 if fo == 0 else None]
                    mw = top(lambda e, fi=fi, fo=fo, bank=bank: e.matmul(
                        bank[:, 0:512], arv(AR_W2 + fi * 1024 + fo * 128, 128),
                        SB[:, HA + fi * 512:HA + (fi + 1) * 512],
                        start=(fi == 0), stop=(fi == NI2 - 1)), waits=ww)
                for fi in range(NI2):
                    ha_guard[fi] = mw
                if not final:
                    st = 4096 + (fo % 2) * 512
                    st2 = 5120 + (fo % 2) * 512
                    msr = B.dma("sync", SF[:, st:st + 512],
                                srcT[fo * 128:(fo + 1) * 128, c0:c0 + 512],
                                waits=[res_guard[fo % 2], lnlast], sem="dB")
                    mv1 = vop(lambda e, bank=bank, st2=st2, fo=fo: e.tensor_scalar(
                        out=SF[:, st2:st2 + 512], in0=bank[:, 0:512],
                        scalar1=vec_b2[:, l * 8 + fo:l * 8 + fo + 1],
                        scalar2=None, op0=ALU.add),
                        waits=[mw, out_dma[fo % 2], hlast])
                    mv2 = vop(lambda e, st=st, st2=st2: e.tensor_tensor(
                        out=SF[:, st2:st2 + 512], in0=SF[:, st2:st2 + 512],
                        in1=SF[:, st:st + 512], op=ALU.add), waits=[msr])
                    w2slot_guard[fo % 2] = mv2
                    res_guard[fo % 2] = mv2
                    out_dma[fo % 2] = B.dma(
                        "sync", dstT[fo * 128:(fo + 1) * 128, c0:c0 + 512],
                        SF[:, st2:st2 + 512], waits=[mv2],
                        sem="dD" if fo % 2 == 0 else "dE")
                else:
                    st = 4096 + (fo % 2) * 512
                    st3 = 5120 + (fo % 2) * 512
                    msr = B.dma("sync", SF[:, st:st + 512],
                                srcT[fo * 128:(fo + 1) * 128, c0:c0 + 512],
                                waits=[res_guard[fo % 2], lnlast], sem="dB")
                    mdn = B.dma("sync", SF[:, st3:st3 + 512],
                                dnT[s][fo * 128:(fo + 1) * 128, c0:c0 + 512],
                                waits=[dn_guard[fo % 2], hlast], sem="dC")
                    ft = SF[:, fo * 512:(fo + 1) * 512]
                    mv1 = vop(lambda e, bank=bank, ft=ft, fo=fo: e.tensor_scalar(
                        out=ft, in0=bank[:, 0:512],
                        scalar1=vec_b2[:, l * 8 + fo:l * 8 + fo + 1],
                        scalar2=None, op0=ALU.add),
                        waits=[mw, fin_guard[fo]] + lnin_guard)
                    mv2 = vop(lambda e, ft=ft, st=st: e.tensor_tensor(
                        out=ft, in0=ft, in1=SF[:, st:st + 512], op=ALU.add),
                        waits=[msr])
                    mv3 = vop(lambda e, ft=ft, st3=st3: e.tensor_tensor(
                        out=ft, in0=ft, in1=SF[:, st3:st3 + 512], op=ALU.add),
                        waits=[mdn])
                    w2slot_guard[fo % 2] = mv3
                    res_guard[fo % 2] = mv3
                    dn_guard[fo % 2] = mv3
                    fin_guard[fo] = mv3
            if final:
                mt_last = None
                for grp in range(4):
                    xs = 4096 + (grp % 2) * 1024
                    mx0 = B.dma("sync", SF[:, xs:xs + 1024],
                                x0[s, c0 + grp * 128:c0 + (grp + 1) * 128, :],
                                waits=[x0_guard[grp % 2], fin_guard[7]], sem="dF")
                    madds = []
                    for f in range(8):
                        slot = DUO[f % 4]
                        mt = top(lambda e, f=f, grp=grp, slot=slot: e.transpose(
                            slot[:, 0:128],
                            SF[:, f * 512 + grp * 128:f * 512 + (grp + 1) * 128],
                            idf[:]),
                            waits=[fin_guard[f], duo_guard[f % 4]])
                        mt_last = mt
                        mad = vop(lambda e, f=f, xs=xs, slot=slot: e.tensor_tensor(
                            out=SF[:, xs + f * 128:xs + (f + 1) * 128],
                            in0=slot[:, 0:128],
                            in1=SF[:, xs + f * 128:xs + (f + 1) * 128],
                            op=ALU.add), waits=[mt, mx0])
                        duo_guard[f % 4] = mad
                        madds.append(mad)
                    x0_guard[grp % 2] = B.dma(
                        "sync", out[s, c0 + grp * 128:c0 + (grp + 1) * 128, :],
                        SF[:, xs:xs + 1024], waits=madds,
                        sem="dG" if grp % 2 == 0 else "dH")
                for f in range(8):
                    fin_guard[f] = [mt_last, x0_guard[1]]
                for i in range(3):
                    lnin_guard[i] = mt_last
                res_guard = [x0_guard[0], x0_guard[1]]
                dn_guard = [x0_guard[0], x0_guard[1]]
        B.barrier()

    ha_guard_src = [None, None]

    ph0()
    fold_consts(0)
    for s_ in range(S):
        s1a(s_)
    if phases >= 2:
        for s_ in range(S):
            s1b_dense(s_)
    if phases >= 3:
        for s_ in range(S):
            s1b_local(s_, 0, xnT[s_])
    if phases >= 4:
        if phases >= 5:
            s2(0)
            g3 = s3_gen(0)
            s2(1, tick=lambda: next(g3, None))
            for _ in g3:
                pass
            s3(1)
        else:
            for s_ in range(S):
                s2(s_)
    if phases >= 6:
        B.dma("gpsimd", SB[:, 2048:6144].rearrange("p (i o) -> p i o", o=1024),
              wo_w[0].rearrange("(i p) o -> p i o", p=128))
        B.barrier()
        for s_ in range(S):
            s4(s_, 0)
        B.barrier()
        if phases >= 7:
            B.dma("gpsimd", arv(AR_W1, 45056).rearrange("p (i o) -> p i o", o=5632),
                  w1p[0].rearrange("(i p) o -> p i o", p=128))
            B.dma("gpsimd", arv(AR_W2, 22528).rearrange("p (i o) -> p i o", o=1024),
                  w2p[0].rearrange("(i p) o -> p i o", p=128))
        for s_ in range(S):
            s5(s_, 0, xnT[s_], xaT[s_])
    if phases >= 7:
        B.barrier()
        for s_ in range(S):
            s6(s_, 0, xaT[s_], xbT[s_], final=False)
    if phases >= 8:
        fold_consts(1)
        B.dma("gpsimd", arv(AR_WQL, 12288).rearrange("p (i o) -> p i o", o=1536),
              wqkv_l[1].rearrange("(i p) o -> p i o", p=128))
        B.barrier()
        for s_ in range(S):
            s1b_local(s_, 1, xbT[s_])
    if phases >= 9:
        B.dma("gpsimd", SB[:, 2048:6144].rearrange("p (i o) -> p i o", o=1024),
              wo_w[1].rearrange("(i p) o -> p i o", p=128))
        B.barrier()
        for s_ in range(S):
            s4(s_, 1)
        B.barrier()
        if phases >= 10:
            B.dma("gpsimd", arv(AR_W1, 45056).rearrange("p (i o) -> p i o", o=5632),
                  w1p[1].rearrange("(i p) o -> p i o", p=128))
            B.dma("gpsimd", arv(AR_W2, 22528).rearrange("p (i o) -> p i o", o=1024),
                  w2p[1].rearrange("(i p) o -> p i o", p=128))
        for s_ in range(S):
            s5(s_, 1, xbT[s_], xaT[s_])
    if phases >= 10:
        B.barrier()
        for s_ in range(S):
            s6(s_, 1, xaT[s_], None, final=True)
    B.wait("sync", [B.now("sync")])
    B.flush()
    es.close()
    nc._bld_sems = {k: v[1] for k, v in B.sem.items()}
    return nc


def _host_layout(inputs):
    f32 = np.float32
    w1 = np.asarray(inputs["L0_ffn_w1"], f32)
    a = np.concatenate([w1[:, :, :HID], np.zeros((2, D, HIDP - HID), f32)], 2)
    g = np.concatenate([w1[:, :, HID:], np.zeros((2, D, HIDP - HID), f32)], 2)
    at = a.reshape(2, D, NI2, 128)
    gt = g.reshape(2, D, NI2, 128)
    w1p_ = np.stack([at, gt], axis=3).reshape(2, D, 2 * HIDP)
    b1 = np.asarray(inputs["L0_ffn_b1"], f32)
    b1a = np.concatenate([b1[:, :HID], np.zeros((2, HIDP - HID), f32)], 1)
    b1g = np.concatenate([b1[:, HID:], np.zeros((2, HIDP - HID), f32)], 1)
    b1t = np.stack([b1a.reshape(2, NI2, 128), b1g.reshape(2, NI2, 128)],
                   axis=2).reshape(2, NF1, 128).transpose(0, 2, 1)
    w2 = np.asarray(inputs["L0_ffn_w2"], f32)
    w2p_ = np.concatenate([w2, np.zeros((2, HIDP - HID, D), f32)], 1)
    qs = np.asarray(inputs["L0_ffn_q_scale"], f32)
    ks = np.asarray(inputs["L0_ffn_k_scale"], f32)
    qsw = np.concatenate([qs[:, 32:], qs[:, :32]], 1)
    ksw = np.concatenate([ks[:, 32:], ks[:, :32]], 1)
    return {
        "wqkv_d": np.ascontiguousarray(np.asarray(inputs["L0_attn_w_qkv"], f32)),
        "wproj": np.ascontiguousarray(np.asarray(inputs["L0_attn_w_proj"], f32)),
        "bprojT": np.ascontiguousarray(
            np.asarray(inputs["L0_attn_b_proj"], f32).reshape(8, 128).T),
        "wqkv_l": np.ascontiguousarray(np.asarray(inputs["L0_ffn_wqkv"], f32)),
        "qsT": np.ascontiguousarray(qs[:, :, None]),
        "ksT": np.ascontiguousarray(ks[:, :, None]),
        "qswT": np.ascontiguousarray(qsw[:, :, None]),
        "kswT": np.ascontiguousarray(ksw[:, :, None]),
        "wo_w": np.ascontiguousarray(np.asarray(inputs["L0_ffn_wo"], f32)),
        "lngT": np.ascontiguousarray(
            np.asarray(inputs["L0_ffn_ln_g"], f32).reshape(2, 8, 128).transpose(0, 2, 1)),
        "lnbT": np.ascontiguousarray(
            np.asarray(inputs["L0_ffn_ln_b"], f32).reshape(2, 8, 128).transpose(0, 2, 1)),
        "w1p": np.ascontiguousarray(w1p_),
        "b1pT": np.ascontiguousarray(b1t),
        "w2p": np.ascontiguousarray(w2p_),
        "b2T": np.ascontiguousarray(
            np.asarray(inputs["L0_ffn_b2"], f32).reshape(2, 8, 128).transpose(0, 2, 1)),
    }


def kernel(**inputs):
    x = np.ascontiguousarray(np.asarray(inputs["L0_attn_x"], np.float32))
    com = _host_layout(inputs)
    nc = build_program()
    in_maps = []
    for c in range(8):
        m = dict(com)
        m["x0"] = np.ascontiguousarray(x[2 * c:2 * c + 2])
        in_maps.append(m)
    res = run_bass_kernel_spmd(nc, in_maps, core_ids=list(range(8)))
    return np.concatenate([res.results[c]["out"] for c in range(8)], axis=0)


# version 18
# speedup vs baseline: 1.0046x; 1.0046x over previous
import contextlib

import numpy as np

import concourse.bass as bass
import concourse.mybir as mybir
from concourse.bass_utils import run_bass_kernel_spmd

F32 = mybir.dt.float32
BF16 = mybir.dt.bfloat16
AF = mybir.ActivationFunctionType
ALU = mybir.AluOpType
AXX = mybir.AxisListType.X

S = 2
T = 4096
D = 1024
NCH = 8
HD = 16
HL = 8
NW = 8
HID = 2730
HIDP = 2816
NF1 = 44
NI2 = 22
LN_EPS = 1e-5
NEG = -1.0e30
LN1E4 = float(np.log(10000.0))
TWO_PI = float(2.0 * np.pi)
MAGIC = 12582912.0

COMPUTE_SEMS = {"tensor", "vector", "scalar", "gpsimdc"}


def _flat(waits):
    o = []
    for w in waits:
        if w is None:
            continue
        if isinstance(w, list):
            o.extend(x for x in w if x is not None)
        else:
            o.append(w)
    return o


class Bld:
    ENGS = ("sync", "tensor", "vector", "scalar", "gpsimd")

    def __init__(self, nc, es):
        self.nc = nc
        self.es = es
        self.ops = {e: [] for e in self.ENGS}
        self.sem = {}
        for name in ("sync", "tensor", "vector", "scalar", "gpsimdd", "gpsimdc"):
            self.sem[name] = [es.enter_context(nc.semaphore("sem_" + name)), 0]
        self.waited = {e: {} for e in self.ENGS}

    def _dsem(self, name):
        if name not in self.sem:
            self.sem[name] = [self.es.enter_context(self.nc.semaphore("sem_" + name)), 0]
        return name

    def _semname(self, eng, is_dma):
        if eng == "gpsimd":
            return "gpsimdd" if is_dma else "gpsimdc"
        return eng

    def _resolve(self, eng, waits):
        best = {}
        for (sn, val) in _flat(waits):
            if val > best.get(sn, 0):
                best[sn] = val
        todo = []
        wd = self.waited[eng]
        for sn, val in best.items():
            if wd.get(sn, -1) >= val:
                continue
            wd[sn] = val
            todo.append((self.sem[sn][0], val))
        return todo

    SELF_SYNC = {"vector", "scalar", "gpsimdc"}

    def emit(self, eng, fn, waits=(), inc=True, is_dma=False, sem=None):
        semname = self._dsem(sem) if (is_dma and sem) else self._semname(eng, is_dma)
        amt = 16 if is_dma else 1
        if semname in self.SELF_SYNC and self.sem[semname][1] > 0:
            waits = list(waits) + [(semname, self.sem[semname][1])]
        marker = None
        incval = None
        if inc:
            self.sem[semname][1] += amt
            incval = self.sem[semname][1]
            marker = (semname, incval)
        todo = self._resolve(eng, waits)
        h = self.sem[semname][0]

        def run(e, fn=fn, todo=todo, h=h, amt=amt, incval=incval):
            for (hh, v) in todo:
                e.wait_ge(hh, v)
            instr = fn(e)
            if incval is not None:
                instr.then_inc(h, amt)

        self.ops[eng].append(run)
        return marker

    def dma(self, eng, out, in_, waits=(), sem=None, **kw):
        return self.emit(eng, lambda e: e.dma_start(out=out, in_=in_, **kw),
                         waits=waits, is_dma=True, sem=sem)

    def wait(self, eng, waits):
        todo = self._resolve(eng, waits)
        if todo:
            def run(e, todo=todo):
                for (hh, v) in todo:
                    e.wait_ge(hh, v)
            self.ops[eng].append(run)

    def now(self, semname):
        return (semname, self.sem[semname][1])

    def barrier(self):
        marks = [self.now(sn) for sn in self.sem]
        for e in self.ENGS:
            self.wait(e, marks)

    def flush(self):
        ops = self.ops
        self.ops = {e: [] for e in self.ENGS}
        with self.nc.Block() as block:
            @block.sync
            def _(e, _o=ops["sync"]):
                for f in _o:
                    f(e)

            @block.tensor
            def _(e, _o=ops["tensor"]):
                for f in _o:
                    f(e)

            @block.vector
            def _(e, _o=ops["vector"]):
                for f in _o:
                    f(e)

            @block.scalar
            def _(e, _o=ops["scalar"]):
                for f in _o:
                    f(e)

            @block.gpsimd
            def _(e, _o=ops["gpsimd"]):
                for f in _o:
                    f(e)


def build_program(dbg=(), phases=99):
    nc = bass.Bass("TRN2", target_bir_lowering=False, debug=False)
    es = contextlib.ExitStack()

    def dram(name, shape, dt=F32, kind="Internal"):
        k = "ExternalOutput" if (name in dbg and kind == "Internal") else kind
        return nc.dram_tensor(name, shape, dt, kind=k).ap()

    x0 = dram("x0", [S, T, D], F32, kind="ExternalInput")
    wqkv_d = dram("wqkv_d", [8, 128, 384], F32, kind="ExternalInput")
    wproj = dram("wproj", [8, 128, 128], F32, kind="ExternalInput")
    bprojT = dram("bprojT", [128, 8], F32, kind="ExternalInput")
    wqkv_l = dram("wqkv_l", [2, D, 1536], F32, kind="ExternalInput")
    qsT = dram("qsT", [2, 64, 1], F32, kind="ExternalInput")
    ksT = dram("ksT", [2, 64, 1], F32, kind="ExternalInput")
    qswT = dram("qswT", [2, 64, 1], F32, kind="ExternalInput")
    kswT = dram("kswT", [2, 64, 1], F32, kind="ExternalInput")
    wo_w = dram("wo_w", [2, 512, D], F32, kind="ExternalInput")
    lngT = dram("lngT", [2, 128, 8], F32, kind="ExternalInput")
    lnbT = dram("lnbT", [2, 128, 8], F32, kind="ExternalInput")
    w1p = dram("w1p", [2, D, 2 * HIDP], F32, kind="ExternalInput")
    b1pT = dram("b1pT", [2, 128, NF1], F32, kind="ExternalInput")
    w2p = dram("w2p", [2, HIDP, D], F32, kind="ExternalInput")
    b2T = dram("b2T", [2, 128, 8], F32, kind="ExternalInput")
    out = dram("out", [S, T, D], F32, kind="ExternalOutput")

    xnT = dram("xnT", [S, D, T])
    qdT = dram("qdT", [S, D, T], BF16)
    kdT = dram("kdT", [S, D, T], BF16)
    vdt = dram("vdt", [S, T, HD * 65], BF16)
    odT = dram("odT", [S, D, T], BF16)
    dnT = dram("dnT", [S, D, T])
    qlT = dram("qlT", [S, 512, T], BF16)
    klT = dram("klT", [S, 512, T], BF16)
    vlt = dram("vlt", [S, T, HL * 65], BF16)
    olT = dram("olT", [S, 512, T], BF16)
    xaT = dram("xaT", [S, D, T])
    xbT = dram("xbT", [S, D, T])
    rcb = dram("rcb", [4, 64, T])
    rcf = dram("rcf", [4, 128, T])
    dbgst = dram("dbgst", [S, 32, 128, 8])
    dbgxh = dram("dbgxh", [S, D, T], BF16)
    dbghg = dram("dbghg", [S, NI2 * 128, T], BF16)
    dbgmu = dram("dbgmu", [S, 2, NCH, 512])

    sbt = lambda name, shape, dt=F32: es.enter_context(nc.sbuf_tensor(name, shape, dt))
    pst = lambda name, shape, dt=F32: es.enter_context(nc.psum_tensor(name, shape, dt))

    AR = sbt("AR", [128, 67584], BF16)
    SF = sbt("SF", [128, 7168], F32)
    SB = sbt("SB", [128, 16384], BF16)
    idf = sbt("idf", [128, 128], F32)
    idb = sbt("idb", [128, 64], BF16)
    idb128 = sbt("idb128", [128, 128], BF16)
    crn_s = sbt("crn_s", [128, 128], BF16)
    crn_e = sbt("crn_e", [128, 128], BF16)
    blkones = sbt("blkones", [128, 128], BF16)
    avgA = sbt("avgA", [128, 128], F32)
    avgB = sbt("avgB", [128, 128], F32)
    sel64 = sbt("sel64", [64, 128], F32)
    selM0 = sbt("selM0", [64, 128], F32)
    selM32 = sbt("selM32", [64, 128], F32)
    onescol = sbt("onescol", [128, 1], BF16)
    cst = sbt("cst", [128, 8], F32)
    vec_bproj = sbt("vec_bproj", [128, 8], F32)
    vec_lng = sbt("vec_lng", [128, 16], F32)
    vec_lnb = sbt("vec_lnb", [128, 16], F32)
    vec_b1 = sbt("vec_b1", [128, 2 * NF1], F32)
    vec_b2 = sbt("vec_b2", [128, 16], F32)
    vec_qs = sbt("vec_qs", [64, 8], F32)
    MUV = sbt("MUV", [64, 512], F32)
    RAB = sbt("RAB", [64, 512], F32)
    MS = sbt("MS", [128, 64], F32)
    CN = sbt("CN", [128, 128], F32)
    sbc = sbt("sbc_", [128, 512], F32)

    CST_EPS, CST_ONE, CST_LNB, CST_G38 = 0, 1, 2, 3

    duoq_t = pst("duoq", [128, 2048], F32)
    duoq = duoq_t[:, 0:2048]
    duo0 = duoq_t[:, 0:1024]
    duo1 = duoq_t[:, 1024:2048]
    pA = pst("pA", [128, 512], F32)
    pB = pst("pB", [128, 512], F32)
    pC = pst("pC", [128, 512], F32)
    ptr = pst("ptr", [128, 1024], BF16)

    DUO = [duo0[:, 0:512], duo0[:, 512:1024], duo1[:, 0:512], duo1[:, 512:1024]]

    B = Bld(nc, es)

    AR_W1 = 0
    AR_W2 = 45056
    AR_K = 0
    AR_V = 32768
    AR_WPJ = 66048
    AR_WQD = 61696
    SB_QB = 0
    SB_PB = 8192
    SB_OST = 10240
    AR_MSK = 0
    AR_WO = 4096
    AR_WQL = 8192
    AR_KW = 20480
    AR_VW = 28672
    AR_QW = 37376
    AR_PBL = 41472
    AR_OWL = 43520

    def arv(off, n, p0=0, p1=128):
        return AR[p0:p1, off:off + n]

    def vop(fn, waits=(), inc=True):
        return B.emit("vector", fn, waits=waits, inc=inc)

    def aop(fn, waits=(), inc=True):
        return B.emit("scalar", fn, waits=waits, inc=inc)

    def top(fn, waits=(), inc=True):
        return B.emit("tensor", fn, waits=waits, inc=inc)

    def gop(fn, waits=(), inc=True):
        return B.emit("gpsimd", fn, waits=waits, inc=inc)

    def ph0():
        gop(lambda e: e.memset(cst[:, CST_EPS:CST_EPS + 1], LN_EPS))
        gop(lambda e: e.memset(cst[:, CST_ONE:CST_ONE + 1], 1.0))
        gop(lambda e: e.memset(cst[:, CST_LNB:CST_LNB + 1], 25.6 / 89.6))
        gop(lambda e: e.memset(cst[:, CST_G38:CST_G38 + 1], 1e-38))
        gop(lambda e: e.memset(onescol[:], 1.0))
        gop(lambda e: e.memset(RAB[:], 1.0))
        gop(lambda e: e.memset(MUV[:], 0.0))
        gop(lambda e: e.memset(idf[:], 1.0))
        gop(lambda e: e.affine_select(out=idf[:], in_=idf[:], pattern=[[-1, 128]],
                                      compare_op=ALU.is_equal, fill=0.0,
                                      base=0, channel_multiplier=1))
        gop(lambda e: e.memset(idb[0:64, :], 1.0))
        midb = gop(lambda e: e.affine_select(out=idb[0:64, :], in_=idb[0:64, :],
                                             pattern=[[-1, 64]],
                                             compare_op=ALU.is_equal, fill=0.0,
                                             base=0, channel_multiplier=1))
        gop(lambda e: e.tensor_copy(out=idb[64:128, :], in_=idb[0:64, :]))
        gop(lambda e: e.memset(idb128[:], 1.0))
        gop(lambda e: e.affine_select(out=idb128[:], in_=idb128[:], pattern=[[-1, 128]],
                                      compare_op=ALU.is_equal, fill=0.0,
                                      base=0, channel_multiplier=1))
        gop(lambda e: e.memset(crn_s[:], 0.0))
        gop(lambda e: e.affine_select(out=crn_s[:], in_=crn_s[:], pattern=[[1, 128]],
                                      compare_op=ALU.is_ge, fill=NEG,
                                      base=0, channel_multiplier=-1))
        gop(lambda e: e.memset(crn_e[:], 0.0))
        gop(lambda e: e.affine_select(out=crn_e[:], in_=crn_e[:], pattern=[[-1, 128]],
                                      compare_op=ALU.is_ge, fill=NEG,
                                      base=0, channel_multiplier=1))
        gop(lambda e: e.memset(blkones[:], 0.0))
        gop(lambda e: e.memset(blkones[0:64, 0:64], 1.0))
        gop(lambda e: e.memset(blkones[64:128, 64:128], 1.0))
        gop(lambda e: e.memset(avgA[:], 0.0))
        gop(lambda e: e.memset(avgA[0:64, :], 1.0 / 64))
        gop(lambda e: e.memset(avgB[:], 0.0))
        gop(lambda e: e.memset(avgB[64:128, :], 1.0 / 64))
        gop(lambda e: e.memset(sel64[:], 0.0))
        gop(lambda e: e.memset(sel64[0:1, 0:64], 1.0))
        gop(lambda e: e.memset(sel64[32:33, 64:128], 1.0))
        gop(lambda e: e.memset(selM0[:], 0.0))
        gop(lambda e: e.memset(selM0[0:1, :], 1.0))
        gop(lambda e: e.memset(selM32[:], 0.0))
        mco = gop(lambda e: e.memset(selM32[32:33, :], 1.0))

        B.dma("sync", vec_bproj[:], bprojT)
        for l in range(2):
            B.dma("sync", vec_lng[:, l * 8:(l + 1) * 8], lngT[l])
            B.dma("sync", vec_lnb[:, l * 8:(l + 1) * 8], lnbT[l])
            B.dma("sync", vec_b1[:, l * NF1:(l + 1) * NF1], b1pT[l])
            B.dma("sync", vec_b2[:, l * 8:(l + 1) * 8], b2T[l])
            B.dma("sync", vec_qs[:, l * 4 + 0:l * 4 + 1], qsT[l])
            B.dma("sync", vec_qs[:, l * 4 + 1:l * 4 + 2], ksT[l])
            B.dma("sync", vec_qs[:, l * 4 + 2:l * 4 + 3], qswT[l])
            B.dma("sync", vec_qs[:, l * 4 + 3:l * 4 + 4], kswT[l])

        B.dma("gpsimd", arv(AR_WPJ, 1024).rearrange("p (g o) -> p g o", o=128),
              wproj.rearrange("g p o -> p g o"))
        B.dma("gpsimd", arv(AR_WQD, 3072).rearrange("p (g o) -> p g o", o=384),
              wqkv_d.rearrange("g p o -> p g o"))
        B.dma("gpsimd", arv(AR_WQL, 12288).rearrange("p (i o) -> p i o", o=1536),
              wqkv_l[0].rearrange("(i p) o -> p i o", p=128))

        tramp = SF[0:1, 0:4096]
        powr = SF[64:65, 0:4096]
        hix = SF[0:1, 4096:4160]
        ivf = SF[0:1, 4160:4224]
        lnb_ = SF[64:65, 4224:4288]
        m_t = gop(lambda e: e.iota(tramp, [[1, 4096]], base=0, channel_multiplier=0,
                                allow_small_or_imprecise_dtypes=True))
        m_p = gop(lambda e: e.iota(powr, [[1, 4096]], base=-2048, channel_multiplier=0,
                                allow_small_or_imprecise_dtypes=True))
        m_h = gop(lambda e: e.iota(hix, [[0, 2], [2, 32]], base=0, channel_multiplier=0,
                                allow_small_or_imprecise_dtypes=True))
        mA1 = aop(lambda e: e.activation(out=ivf, in_=hix, func=AF.Exp,
                                         scale=-LN1E4 / 64.0), waits=[m_h])
        mA2 = aop(lambda e: e.activation(out=lnb_, in_=hix, func=AF.Ln,
                                         bias=cst[0:1, CST_LNB:CST_LNB + 1],
                                         scale=1.0 / 89.6), waits=[m_h, mco])
        mpw = vop(lambda e: e.tensor_scalar(out=powr, in0=powr, scalar1=1.0 / 256,
                                            scalar2=None, op0=ALU.mult), waits=[m_p])

        G_, K_, R_, S_, C_ = 4608, 5120, 5632, 6144, 6656
        prev = []
        pab_guard = []
        for j in range(8):
            c0 = j * 512
            mf = top(lambda e, c0=c0: e.matmul(pA[0:64, :], ivf,
                                               SF[0:1, c0:c0 + 512],
                                               start=True, stop=True),
                     waits=[mA1, m_t] + pab_guard)
            ms = top(lambda e, c0=c0: e.matmul(pB[0:64, :], lnb_,
                                               SF[64:65, c0:c0 + 512],
                                               start=True, stop=True),
                     waits=[mA2, mpw])
            g_ = SF[0:64, G_:G_ + 512]
            k_ = SF[0:64, K_:K_ + 512]
            r_ = SF[0:64, R_:R_ + 512]
            s_ = SF[0:64, S_:S_ + 512]
            c_ = SF[0:64, C_:C_ + 512]
            m1 = vop(lambda e, g_=g_: e.tensor_scalar(out=g_, in0=pA[0:64, :],
                                                      scalar1=1.0 / TWO_PI, scalar2=None,
                                                      op0=ALU.mult),
                     waits=[mf] + prev)
            vop(lambda e, g_=g_, k_=k_: e.tensor_scalar(out=k_, in0=g_, scalar1=MAGIC,
                                                        scalar2=-MAGIC, op0=ALU.add,
                                                        op1=ALU.add))
            vop(lambda e, g_=g_, k_=k_, r_=r_: e.tensor_tensor(out=r_, in0=g_, in1=k_,
                                                               op=ALU.subtract))
            m4 = vop(lambda e, r_=r_: e.tensor_scalar(out=r_, in0=r_, scalar1=TWO_PI,
                                                      scalar2=None, op0=ALU.mult))
            msin = aop(lambda e, r_=r_, s_=s_: e.activation(out=s_, in_=r_, func=AF.Sin),
                       waits=[m4] + prev)
            aop(lambda e, r_=r_, c_=c_: e.activation(out=c_, in_=r_, func=AF.Sin,
                                                     scale=0.5))
            mh2 = aop(lambda e, c_=c_: e.activation(out=c_, in_=c_, func=AF.Square))
            mcos = vop(lambda e, c_=c_: e.tensor_scalar(out=c_, in0=c_, scalar1=-2.0,
                                                        scalar2=1.0, op0=ALU.mult,
                                                        op1=ALU.add), waits=[mh2])
            sg = SF[0:64, G_:G_ + 512]
            si = SF[0:64, K_:K_ + 512]
            msg = aop(lambda e, sg=sg: e.activation(out=sg, in_=pB[0:64, :],
                                                    func=AF.Exp), waits=[ms] + prev)
            msi = aop(lambda e, si=si: e.activation(out=si, in_=pB[0:64, :],
                                                    func=AF.Exp, scale=-1.0))
            pab_guard = [msi, m1]
            cs = SF[0:64, R_:R_ + 512]
            ssx = SF[0:64, G_:G_ + 512]
            ci = SF[0:64, C_:C_ + 512]
            sii = SF[0:64, S_:S_ + 512]
            w1_ = vop(lambda e, c_=c_, sg=sg, cs=cs: e.tensor_tensor(
                out=cs, in0=c_, in1=sg, op=ALU.mult), waits=[mcos, msg])
            w2_ = vop(lambda e, s_=s_, sg=sg, ssx=ssx: e.tensor_tensor(
                out=ssx, in0=s_, in1=sg, op=ALU.mult), waits=[msin])
            w3_ = vop(lambda e, c_=c_, si=si, ci=ci: e.tensor_tensor(
                out=ci, in0=c_, in1=si, op=ALU.mult), waits=[msi])
            w4_ = vop(lambda e, s_=s_, si=si, sii=sii: e.tensor_tensor(
                out=sii, in0=s_, in1=si, op=ALU.mult))
            f1 = vop(lambda e: e.tensor_scalar(out=SF[0:32, G_:G_ + 512],
                                               in0=SF[0:32, G_:G_ + 512],
                                               scalar1=-1.0, scalar2=None, op0=ALU.mult))
            f2 = vop(lambda e: e.tensor_scalar(out=SF[0:32, S_:S_ + 512],
                                               in0=SF[0:32, S_:S_ + 512],
                                               scalar1=-1.0, scalar2=None, op0=ALU.mult))
            prev = [
                B.dma("sync", rcb[0][0:64, c0:c0 + 512], cs, waits=[w1_], sem="dA"),
                B.dma("sync", rcb[1][0:64, c0:c0 + 512], ssx, waits=[f1], sem="dA"),
                B.dma("sync", rcb[2][0:64, c0:c0 + 512], ci, waits=[w3_], sem="dA"),
                B.dma("sync", rcb[3][0:64, c0:c0 + 512], sii, waits=[f2], sem="dA"),
            ]
        B.barrier()

    def fold_consts(l):
        spec = [(0, 0, 8.0), (1, 2, 8.0), (2, 1, 1.0), (3, 3, 1.0)]
        guards = [[] for _ in range(4)]
        i = 0
        for (k, col, mul) in spec:
            vc = vec_qs[:, l * 4 + col:l * 4 + col + 1]
            for j in range(8):
                c0 = j * 512
                sl = i % 4
                tile = SF[0:64, 4608 + sl * 512:4608 + (sl + 1) * 512]
                ml = B.dma("sync", tile, rcb[k][0:64, c0:c0 + 512],
                           waits=guards[sl], sem="dA" if sl % 2 == 0 else "dB")
                mm = vop(lambda e, vc=vc, mul=mul, tile=tile: e.tensor_scalar(
                    out=tile, in0=tile, scalar1=vc, scalar2=mul,
                    op0=ALU.mult, op1=ALU.mult), waits=[ml])
                guards[sl] = [
                    B.dma("sync", rcf[k][0:64, c0:c0 + 512], tile, waits=[mm],
                          sem="dC" if sl % 2 == 0 else "dD"),
                    B.dma("sync", rcf[k][64:128, c0:c0 + 512], tile, waits=[mm],
                          sem="dC" if sl % 2 == 0 else "dD"),
                ]
                i += 1
        B.barrier()

    def build_masks():
        for kc in range(8):
            mt = arv(AR_MSK + kc * 512, 512)
            gop(lambda e, mt=mt: e.memset(mt, 0.0))
            gop(lambda e, mt=mt, kc=kc: e.affine_select(
                out=mt, in_=mt, pattern=[[-1, 512]], compare_op=ALU.is_ge,
                fill=NEG, base=128 * kc, channel_multiplier=1))
            gop(lambda e, mt=mt, kc=kc: e.affine_select(
                out=mt, in_=mt, pattern=[[1, 512]], compare_op=ALU.is_ge,
                fill=NEG, base=512 - 128 * kc, channel_multiplier=-1))
        B.barrier()

    def s1a(s):
        X0A, X0B, XN, ST, EST, SQS = 0, 1024, 2048, 3072, 4096, 5120
        ESTB = 6144
        x0_guard = [[], []]
        bank_guard = [None, None]
        est_guard = [None, None]
        xn_guard = None
        m3_prev = None
        for g in range(32):
            sl = g % 2
            base = X0A if sl == 0 else X0B
            x0t = SF[:, base:base + 1024]
            ml = B.dma("sync", x0t, x0[s, g * 128:(g + 1) * 128, :],
                       waits=x0_guard[sl], sem="dA" if sl == 0 else "dB")
            msum = vop(lambda e, x0t=x0t: e.tensor_reduce(
                out=SF[:, ST:ST + 1], in_=x0t, axis=AXX, op=ALU.add), waits=[ml])
            msq = aop(lambda e, x0t=x0t: e.activation(
                out=SF[:, SQS:SQS + 1024], in_=x0t, func=AF.Square,
                accum_out=SF[:, ST + 1:ST + 2]), waits=[ml, m3_prev])
            m1 = vop(lambda e: e.tensor_scalar(out=SF[:, ST + 2:ST + 3],
                                               in0=SF[:, ST:ST + 1],
                                               scalar1=1.0 / D, scalar2=None,
                                               op0=ALU.mult), waits=[msum])
            vop(lambda e: e.tensor_tensor(out=SF[:, ST + 3:ST + 4],
                                          in0=SF[:, ST + 2:ST + 3],
                                          in1=SF[:, ST + 2:ST + 3], op=ALU.mult))
            m3 = vop(lambda e: e.tensor_scalar(out=SF[:, ST + 4:ST + 5],
                                               in0=SF[:, ST + 1:ST + 2],
                                               scalar1=1.0 / D, scalar2=None,
                                               op0=ALU.mult), waits=[msq])
            m3_prev = m3
            m4 = vop(lambda e: e.tensor_tensor(out=SF[:, ST + 4:ST + 5],
                                               in0=SF[:, ST + 4:ST + 5],
                                               in1=SF[:, ST + 3:ST + 4],
                                               op=ALU.subtract))
            aop(lambda e: e.activation(out=SF[:, ST + 5:ST + 6], in_=SF[:, ST + 4:ST + 5],
                                       func=AF.Ln, bias=cst[:, CST_EPS:CST_EPS + 1]),
                waits=[m4])
            m6 = aop(lambda e: e.activation(out=SF[:, ST + 6:ST + 7],
                                            in_=SF[:, ST + 5:ST + 6],
                                            func=AF.Exp, scale=-0.5))
            xnt = SF[:, XN:XN + 1024]
            m7 = vop(lambda e, x0t=x0t, xnt=xnt: e.tensor_scalar(
                out=xnt, in0=x0t, scalar1=SF[:, ST + 2:ST + 3],
                scalar2=SF[:, ST + 6:ST + 7], op0=ALU.subtract, op1=ALU.mult),
                waits=[m6, xn_guard])
            x0_guard[sl] = [m7, msq]
            if "dbgst" in dbg:
                B.dma("sync", dbgst[s][g], SF[:, ST:ST + 8], waits=[m7])
            eb = EST if g % 2 == 0 else ESTB
            mc = None
            for f in range(8):
                bk = pA if f % 2 == 0 else pB
                mt = top(lambda e, xnt=xnt, f=f, bk=bk: e.transpose(
                    bk[:, 0:128], xnt[:, f * 128:(f + 1) * 128], idf[:]),
                    waits=[m7, bank_guard[f % 2]])
                if f == 7:
                    xn_guard = mt
                mc = vop(lambda e, f=f, bk=bk, eb=eb: e.tensor_copy(
                    out=SF[:, eb + f * 128:eb + (f + 1) * 128], in_=bk[:, 0:128]),
                    waits=[mt, est_guard[g % 2] if f == 0 else None])
                bank_guard[f % 2] = mc
            est_guard[g % 2] = B.dma(
                "gpsimd",
                xnT[s][:, g * 128:(g + 1) * 128].rearrange("(f p) c -> p f c", p=128),
                SF[:, eb:eb + 1024].rearrange("p (f c) -> p f c", c=128),
                waits=[mc], sem="gC" if g % 2 == 0 else "gD")
        B.barrier()

    def s1b_dense(s):
        XB0, QST, KST, VT0, VST = 0, 4096, 4608, 5120, 13312
        XBA = 0
        QKA = 20480
        slot_guard = [None] * 4
        xb_guard = [[None] * 8, [None] * 8]
        stg_dma = {(0, 0): None, (0, 1): None, (1, 0): None, (1, 1): None}
        vt_guard = [None] * 8
        vst_dma = [None, None]
        ptr_guard = None
        for c in range(NCH):
            c0 = c * 512
            xb = XBA + (c % 2) * 4096
            mx = []
            for f in range(8):
                mx.append(B.dma("gpsimd", arv(xb + f * 512, 512),
                                xnT[s][f * 128:(f + 1) * 128, c0:c0 + 512],
                                waits=[xb_guard[c % 2][f]],
                                sem="gA" if c % 2 == 0 else "gB"))
            half = {}
            vt_cp = {}
            for g in range(8):
                for m in range(3):
                    idx = g * 3 + m
                    slot = DUO[idx % 4]
                    mqk = top(lambda e, g=g, m=m, slot=slot, xb=xb: e.matmul(
                        slot, arv(AR_WQD + g * 384 + m * 128, 128),
                        arv(xb + g * 512, 512),
                        start=True, stop=True),
                        waits=[mx[-1], slot_guard[idx % 4]])
                    xb_guard[c % 2][g] = mqk
                    last_cp = None
                    kind = (idx * 128) // 1024
                    if kind == 2:
                        for h2 in range(2):
                            fi = (idx * 128 + h2 * 64) % 1024
                            hh = fi // 64
                            src = slot[h2 * 64:(h2 + 1) * 64, :]
                            dst = SB[0:64, VT0 + hh * 512:VT0 + (hh + 1) * 512]
                            mcp = vop(lambda e, dst=dst, src=src: e.tensor_copy(
                                out=dst, in_=src), waits=[mqk, vt_guard[hh % 8]])
                            vt_cp[hh] = mcp
                            last_cp = mcp
                    else:
                        pr = ((idx * 128) % 1024) // 128
                        par = pr % 2
                        st = QKA + kind * 2048 + par * 512
                        mcp = vop(lambda e, st=st, slot=slot: e.tensor_copy(
                            out=arv(st, 512), in_=slot),
                            waits=[mqk, stg_dma[(kind, par)]])
                        tgt = (qdT if kind == 0 else kdT)[s][
                            pr * 128:(pr + 1) * 128, c0:c0 + 512]
                        stg_dma[(kind, par)] = B.dma(
                            "sync", tgt, arv(st, 512), waits=[mcp],
                            sem="dC" if kind == 0 else "dD")
                        last_cp = mcp
                    slot_guard[idx % 4] = last_cp
            for grp in range(4):
                vs = VST + (grp % 2) * 1040
                mms = gop(lambda e, vs=vs: e.memset(
                    SB[:, vs:vs + 1040].rearrange("p (h o) -> p h o", o=65)[:, :, 64:65],
                    1.0), waits=[vst_dma[grp % 2]])
                trs = None
                for h in range(16):
                    trs = top(lambda e, grp=grp, h=h: e.transpose(
                        ptr[:, h * 64:(h + 1) * 64],
                        SB[0:64, VT0 + h * 512 + grp * 128:VT0 + h * 512 + (grp + 1) * 128],
                        idb[0:64, :]),
                        waits=[vt_cp[h], ptr_guard])
                mcv = vop(lambda e, vs=vs: e.tensor_copy(
                    out=SB[:, vs:vs + 1040].rearrange("p (h o) -> p h o", o=65)[:, :, 0:64],
                    in_=ptr[:].rearrange("p (h o) -> p h o", o=64)),
                    waits=[trs, mms])
                ptr_guard = mcv
                vst_dma[grp % 2] = B.dma(
                    "sync", vdt[s][c0 + grp * 128:c0 + (grp + 1) * 128, :],
                    SB[:, vs:vs + 1040], waits=[mcv],
                    sem="dE" if grp % 2 == 0 else "dF")
            for pr in range(8):
                vt_guard[pr] = ptr_guard
        B.barrier()

    def s1b_local(s, l, src):
        XB0, VT0, VST, SQB, UHB = 0, 4096, 8192, 9232, 10256
        RC0, U_, M1_, M2_, RN_, RN2 = 0, 2048, 2560, 3072, 3584, 4096
        slot_guard = [None] * 4
        xb_guard = [None] * 8
        rc_guard = []
        sq_guard = [None, None]
        uh_guard = [None, None]
        rn2_guard = None
        vt_guard = [None] * 4
        vst_dma = [None, None]
        ptr_guard = None
        pc_guard = None
        ms_marks = []
        for c in range(NCH):
            c0 = c * 512
            mx = []
            for f in range(8):
                dst = SB[:, XB0 + f * 512:XB0 + (f + 1) * 512]
                gsem = "gA" if c % 2 == 0 else "gB"
                if f >= 4 and c == 0:
                    mz = gop(lambda e, dst=dst: e.memset(dst[:, 0:1], 0.0),
                             waits=[xb_guard[f]])
                    mx.append(B.dma("gpsimd", dst[:, 1:512],
                                    src[f * 128:(f + 1) * 128, 0:511], waits=[mz],
                                    sem=gsem))
                elif f >= 4:
                    mx.append(B.dma("gpsimd", dst,
                                    src[f * 128:(f + 1) * 128, c0 - 1:c0 + 511],
                                    waits=[xb_guard[f]], sem=gsem))
                else:
                    mx.append(B.dma("gpsimd", dst,
                                    src[f * 128:(f + 1) * 128, c0:c0 + 512],
                                    waits=[xb_guard[f]], sem=gsem))
            mrc = []
            for k in range(4):
                mrc.append(B.dma("sync", SF[:, RC0 + k * 512:RC0 + (k + 1) * 512],
                                 rcf[k][:, c0:c0 + 512], waits=rc_guard, sem="dA"))
            rc_guard = []
            sidx = 0
            pend = [None]

            def emit_tail():
                nonlocal rn2_guard, pc_guard
                if pend[0] is None:
                    return
                (si, slot, msq, kind_, pr_, sqb_) = pend[0]
                pend[0] = None
                mss = top(lambda e, sqb_=sqb_: e.matmul(
                    pC[:, 0:512], blkones[:], sqb_, start=True, stop=True),
                    waits=[msq, pc_guard])
                sq_guard[si % 2] = mss
                mln = aop(lambda e: e.activation(out=SF[:, RN_:RN_ + 512],
                                                 in_=pC[:, 0:512], func=AF.Ln),
                          waits=[mss])
                pc_guard = mln
                mrn = aop(lambda e: e.activation(out=SF[:, RN2:RN2 + 512],
                                                 in_=SF[:, RN_:RN_ + 512],
                                                 func=AF.Exp, scale=-0.5),
                          waits=[rn2_guard])
                mu = vop(lambda e, slot=slot: e.tensor_tensor(
                    out=SF[:, U_:U_ + 512], in0=slot, in1=SF[:, RN2:RN2 + 512],
                    op=ALU.mult), waits=[mrn])
                rn2_guard = mu
                slot_guard[si % 4] = mu
                A0 = RC0 + (0 if kind_ == 0 else 2) * 512
                B0 = RC0 + (1 if kind_ == 0 else 3) * 512
                mm1 = vop(lambda e, A0=A0: e.tensor_tensor(
                    out=SF[:, M1_:M1_ + 512], in0=SF[:, U_:U_ + 512],
                    in1=SF[:, A0:A0 + 512], op=ALU.mult), waits=mrc)
                for qtr in range(4):
                    b0 = qtr * 32
                    sq_ = 32 * ((qtr // 2) * 2 + (1 - qtr % 2))
                    vop(lambda e, b0=b0, sq_=sq_: e.tensor_copy(
                        out=SF[b0:b0 + 32, M2_:M2_ + 512],
                        in_=SF[sq_:sq_ + 32, U_:U_ + 512]))
                mq4 = vop(lambda e, B0=B0: e.tensor_tensor(
                    out=SF[:, M2_:M2_ + 512], in0=SF[:, M2_:M2_ + 512],
                    in1=SF[:, B0:B0 + 512], op=ALU.mult))
                rc_guard.extend([mm1, mq4])
                uh = SB[:, UHB + (si % 2) * 512:UHB + (si % 2) * 512 + 512]
                muh = vop(lambda e, uh=uh: e.tensor_tensor(
                    out=uh, in0=SF[:, M1_:M1_ + 512], in1=SF[:, M2_:M2_ + 512],
                    op=ALU.add), waits=[uh_guard[si % 2]])
                tgt = (qlT if kind_ == 0 else klT)[s][pr_ * 128:(pr_ + 1) * 128,
                                                      c0:c0 + 512]
                mdq = B.dma("sync", tgt, uh, waits=[muh],
                            sem="dC" if si % 2 == 0 else "dD")
                uh_guard[si % 2] = [mdq]

            for pr in range(4):
                for kind in range(2):
                    fo = kind * 4 + pr
                    slot = DUO[sidx % 4]
                    mons = None
                    for fi in range(8):
                        w = ([slot_guard[sidx % 4]] + mx) if fi == 0 else []
                        mons = top(lambda e, fi=fi, fo=fo, slot=slot: e.matmul(
                            slot, arv(AR_WQL + fi * 1536 + fo * 128, 128),
                            SB[:, XB0 + fi * 512:XB0 + (fi + 1) * 512],
                            start=(fi == 0), stop=(fi == 7)), waits=w)
                    for f in range(8):
                        xb_guard[f] = mons
                    sqb = SB[:, SQB + (sidx % 2) * 512:SQB + (sidx % 2) * 512 + 512]
                    msq = aop(lambda e, slot=slot, sqb=sqb: e.activation(
                        out=sqb, in_=slot, func=AF.Square),
                        waits=[mons, sq_guard[sidx % 2]])
                    emit_tail()
                    pend[0] = (sidx, slot, msq, kind, pr, sqb)
                    sidx += 1
            emit_tail()
            vt_cp = {}
            for pr in range(4):
                fo = 8 + pr
                slot = DUO[pr % 4]
                mons = None
                for fi in range(8):
                    w = ([slot_guard[pr % 4]] + mx) if fi == 0 else []
                    mons = top(lambda e, fi=fi, fo=fo, slot=slot: e.matmul(
                        slot, arv(AR_WQL + fi * 1536 + fo * 128, 128),
                        SB[:, XB0 + fi * 512:XB0 + (fi + 1) * 512],
                        start=(fi == 0), stop=(fi == 7)), waits=w)
                for f in range(8):
                    xb_guard[f] = mons
                mva = vop(lambda e, pr=pr, slot=slot: e.tensor_copy(
                    out=SB[0:64, VT0 + (2 * pr) * 512:VT0 + (2 * pr) * 512 + 512],
                    in_=slot[0:64, :]),
                    waits=[mons, vt_guard[pr]])
                mv = vop(lambda e, pr=pr, slot=slot: e.tensor_copy(
                    out=SB[0:64, VT0 + (2 * pr + 1) * 512:VT0 + (2 * pr + 1) * 512 + 512],
                    in_=slot[64:128, :]))
                slot_guard[pr % 4] = mv
                vt_cp[2 * pr] = mva
                vt_cp[2 * pr + 1] = mv
            for grp in range(4):
                vs = VST + (grp % 2) * 520
                mms = gop(lambda e, vs=vs: e.memset(
                    SB[:, vs:vs + 520].rearrange("p (h o) -> p h o", o=65)[:, :, 64:65],
                    1.0), waits=[vst_dma[grp % 2]])
                trs = None
                for h in range(8):
                    trs = top(lambda e, grp=grp, h=h: e.transpose(
                        ptr[:, h * 64:(h + 1) * 64],
                        SB[0:64, VT0 + h * 512 + grp * 128:VT0 + h * 512 + (grp + 1) * 128],
                        idb[0:64, :]),
                        waits=[vt_cp[h], ptr_guard])
                mcv = vop(lambda e, vs=vs: e.tensor_copy(
                    out=SB[:, vs:vs + 520].rearrange("p (h o) -> p h o", o=65)[:, :, 0:64],
                    in_=ptr[:, 0:512].rearrange("p (h o) -> p h o", o=64)),
                    waits=[trs, mms])
                ptr_guard = mcv
                vst_dma[grp % 2] = B.dma(
                    "sync", vlt[s][c0 + grp * 128:c0 + (grp + 1) * 128, :],
                    SB[:, vs:vs + 520], waits=[mcv],
                    sem="dE" if grp % 2 == 0 else "dF")
            for pr in range(4):
                vt_guard[pr] = ptr_guard
        B.barrier()

    def s2(s, tick=None):
        mk = []
        for pr in range(8):
            mk.append(B.dma("sync", arv(AR_K + pr * 4096, 4096),
                            kdT[s][pr * 128:(pr + 1) * 128, :], sem="dA"))
        for kc in range(32):
            mk.append(B.dma("sync", arv(AR_V + kc * 1040, 1040),
                            vdt[s][kc * 128:(kc + 1) * 128, :], sem="dA"))
        mk = mk[-1:]
        qb_guard = [None, None]
        pv_guard = [None, None]
        ost_dma = [None, None]
        pc_guard = None
        rab_guard = None
        prev_exp = [None, None]
        prev_pv = [None, None]
        pend_norm = None
        STG0 = 5120
        for pr in range(8):
            qb = SB_QB + (pr % 2) * 4096
            mq = B.dma("sync", SB[:, qb:qb + 4096], qdT[s][pr * 128:(pr + 1) * 128, :],
                       waits=[qb_guard[pr % 2]], sem="dB" if pr % 2 == 0 else "dC")
            last_mm = None
            for qc in range(8):
                q0 = qc * 512
                mexp = {}
                mpvk = {}

                def emit_pv(kc, pr=pr, q0=q0):
                    pb = SB_PB + (kc % 2) * 1024
                    wl3 = [mexp[kc]]
                    wlB = []
                    if kc == 0:
                        wl3.append(pv_guard[0])
                        wlB.append(pv_guard[1])
                    top(lambda e, kc=kc, pr=pr, pb=pb: e.matmul(
                        pA[0:65, :], arv(AR_V + kc * 1040 + (2 * pr) * 65, 65),
                        SB[:, pb:pb + 512], start=(kc == 0), stop=(kc == 31)),
                        waits=wl3)
                    mpv = top(lambda e, kc=kc, pr=pr, pb=pb: e.matmul(
                        pB[0:65, :], arv(AR_V + kc * 1040 + (2 * pr + 1) * 65, 65),
                        SB[:, pb + 512:pb + 1024], start=(kc == 0), stop=(kc == 31)),
                        waits=wlB)
                    mpvk[kc] = mpv
                    return mpv

                for kc in range(32):
                    if kc == 8 and pend_norm is not None:
                        pend_norm()
                        pend_norm = None
                    duo = duo0 if kc % 2 == 0 else duo1
                    wl = [mexp.get(kc - 2) if kc >= 2 else prev_exp[kc]]
                    if kc == 0:
                        wl += mk + [mq]
                    top(lambda e, pr=pr, kc=kc, qb=qb, q0=q0, duo=duo: e.matmul(
                        duo[:, 0:512],
                        arv(AR_K + pr * 4096 + kc * 128, 128, 0, 64),
                        SB[0:64, qb + q0:qb + q0 + 512],
                        start=True, stop=True, tile_position=(0, 0)), waits=wl)
                    mqk = top(lambda e, pr=pr, kc=kc, qb=qb, q0=q0, duo=duo: e.matmul(
                        duo[:, 512:1024],
                        arv(AR_K + pr * 4096 + kc * 128, 128, 64, 128),
                        SB[64:128, qb + q0:qb + q0 + 512],
                        start=True, stop=True, tile_position=(64, 0)))
                    pb = SB_PB + (kc % 2) * 1024
                    me = aop(lambda e, duo=duo, pb=pb: e.activation(
                        out=SB[:, pb:pb + 1024], in_=duo[:], func=AF.Exp, scale=0.125),
                        waits=[mqk, mpvk.get(kc - 2) if kc >= 2 else prev_pv[kc]])
                    mexp[kc] = me
                    if kc >= 1:
                        last_mm = emit_pv(kc - 1)
                last_mm = emit_pv(31)
                prev_exp = [mexp[30], mexp[31]]
                prev_pv = [mpvk[30], mpvk[31]]
                stg = STG0 + (qc % 2) * 512
                cA = vop(lambda e, stg=stg: e.tensor_copy(
                    out=SF[0:64, stg:stg + 512], in_=pA[0:64, :]),
                    waits=[last_mm])
                rA = vop(lambda e: e.tensor_copy(out=RAB[0:1, :], in_=pA[64:65, :]),
                         waits=[rab_guard])
                cB = vop(lambda e, stg=stg: e.tensor_copy(
                    out=SF[64:128, stg:stg + 512], in_=pB[0:64, :]))
                rB = vop(lambda e: e.tensor_copy(out=RAB[32:33, :], in_=pB[64:65, :]))
                pv_guard = [rA, rB]

                def mk_norm(qc=qc, stg=stg, q0=q0, pr=pr):
                    nonlocal pc_guard, rab_guard
                    rc = vop(lambda e: e.reciprocal(out=RAB[0:33, :], in_=RAB[0:33, :]))
                    mbc = top(lambda e: e.matmul(pC[:, 0:512], sel64[:], RAB[:],
                                                 start=True, stop=True),
                              waits=[rc, pc_guard])
                    rab_guard = mbc
                    ost = SB_OST + (qc % 2) * 512
                    o1 = vop(lambda e, ost=ost, stg=stg: e.tensor_tensor(
                        out=SB[0:64, ost:ost + 512], in0=SF[0:64, stg:stg + 512],
                        in1=pC[0:64, :], op=ALU.mult),
                        waits=[mbc, ost_dma[qc % 2]])
                    o2 = vop(lambda e, ost=ost, stg=stg: e.tensor_tensor(
                        out=SB[64:128, ost:ost + 512], in0=SF[64:128, stg:stg + 512],
                        in1=pC[64:128, :], op=ALU.mult))
                    pc_guard = o2
                    ost_dma[qc % 2] = B.dma(
                        "sync", odT[s][pr * 128:(pr + 1) * 128, q0:q0 + 512],
                        SB[:, ost:ost + 512], waits=[o1, o2],
                        sem="dD" if qc % 2 == 0 else "dE")
                pend_norm = mk_norm
            qb_guard[pr % 2] = last_mm
            if tick is not None:
                tick()
        if pend_norm is not None:
            pend_norm()
            pend_norm = None
        B.barrier()

    def s3_gen(s):
        OD0 = 12288
        pv3 = ptr[:].bitcast(F32)
        od_guard = [None] * 8
        slot_guard = None
        stg_dma = [None, None]
        pend3 = None
        for c in range(NCH):
            c0 = c * 512
            for h in range(2):
                mo = []
                for g4 in range(4):
                    g = h * 4 + g4
                    mo.append(B.dma("gpsimd",
                                    SB[:, OD0 + g * 512:OD0 + (g + 1) * 512],
                                    odT[s][g * 128:(g + 1) * 128, c0:c0 + 512],
                                    waits=[od_guard[g]], sem="dG"))
                cur = (mo[-1], c0, h)
                if pend3 is not None:
                    (mld, pc0, ph) = pend3
                    for g4 in range(4):
                        g = ph * 4 + g4
                        mm = top(lambda e, g=g: e.matmul(
                            pv3, arv(AR_WPJ + g * 128, 128),
                            SB[:, OD0 + g * 512:OD0 + (g + 1) * 512],
                            start=True, stop=True), waits=[mld, slot_guard])
                        od_guard[g] = mm
                        st = 4096 + (g % 2) * 512
                        me = vop(lambda e, st=st, g=g, pc0=pc0: e.tensor_scalar(
                            out=SF[:, st:st + 512], in0=pv3,
                            scalar1=vec_bproj[:, g:g + 1],
                            scalar2=None, op0=ALU.add), waits=[mm, stg_dma[g % 2]])
                        slot_guard = me
                        stg_dma[g % 2] = B.dma(
                            "gpsimd", dnT[s][g * 128:(g + 1) * 128, pc0:pc0 + 512],
                            SF[:, st:st + 512], waits=[me], sem="dH")
                pend3 = cur
            yield
        if pend3 is not None:
            (mld, pc0, ph) = pend3
            for g4 in range(4):
                g = ph * 4 + g4
                mm = top(lambda e, g=g: e.matmul(
                    pv3, arv(AR_WPJ + g * 128, 128),
                    SB[:, OD0 + g * 512:OD0 + (g + 1) * 512],
                    start=True, stop=True), waits=[mld, slot_guard])
                od_guard[g] = mm
                st = 4096 + (g % 2) * 512
                me = vop(lambda e, st=st, g=g, pc0=pc0: e.tensor_scalar(
                    out=SF[:, st:st + 512], in0=pv3,
                    scalar1=vec_bproj[:, g:g + 1],
                    scalar2=None, op0=ALU.add), waits=[mm, stg_dma[g % 2]])
                slot_guard = me
                stg_dma[g % 2] = B.dma(
                    "gpsimd", dnT[s][g * 128:(g + 1) * 128, pc0:pc0 + 512],
                    SF[:, st:st + 512], waits=[me], sem="dH")
        yield

    def s3(s):
        for _ in s3_gen(s):
            pass
        B.barrier()

    def s4(s, l):
        PST = AR_PBL
        pv_guard = [None, None]
        pc_guard = None
        rab_guard = None
        win_guard = [None, None]
        ptr_guard = None
        stage_guard = [None, None]
        mx_guard = [None] * 8
        last_exp = [None, None]
        pend_norm = None
        for w in range(NW):
            kws = AR_KW + (w % 2) * 4096
            vws = AR_VW + (w % 2) * 4160
            qws = AR_QW + (w % 2) * 2048
            mks = []
            wg = [win_guard[w % 2]]
            wsem = "dA" if w % 2 == 0 else "dB"
            for pr in range(4):
                if w == 0:
                    mks.append(B.dma("sync", arv(kws + pr * 1024 + 512, 512),
                                     klT[s][pr * 128:(pr + 1) * 128, 0:512], waits=wg,
                                     sem=wsem))
                else:
                    mks.append(B.dma("sync", arv(kws + pr * 1024, 1024),
                                     klT[s][pr * 128:(pr + 1) * 128,
                                            (w - 1) * 512:(w + 1) * 512], waits=wg,
                                     sem=wsem))
            kc_lo0 = 4 if w == 0 else 0
            for kc in range(kc_lo0, 8):
                t0 = (w - 1) * 512 + kc * 128
                mks.append(B.dma("sync", arv(vws + kc * 520, 520),
                                 vlt[s][t0:t0 + 128, :], waits=wg, sem=wsem))
            for pr in range(4):
                mks.append(B.dma("sync", arv(qws + pr * 512, 512),
                                 qlT[s][pr * 128:(pr + 1) * 128,
                                        w * 512:(w + 1) * 512], waits=wg, sem=wsem))
            mks = mks[-1:]
            last_mm = None
            pend = [None]

            def emit_back(vws=vws):
                nonlocal last_mm, ptr_guard
                if pend[0] is None:
                    return
                (qt_, c_lo_, nch_, exps, pbufs, first_pv) = pend[0]
                pend[0] = None
                for hb in range(2):
                    pv = pA if hb == 0 else pB
                    trs = None
                    for j in range(nch_):
                        trs = top(lambda e, j=j, pbuf=pbufs[hb]: e.transpose(
                            ptr[:, j * 128:(j + 1) * 128],
                            pbuf[:, j * 128:(j + 1) * 128], idb128[:]),
                            waits=[exps[hb], ptr_guard])
                    mx_guard[hb * 2 + (qt_ % 2)] = trs
                    mev = vop(lambda e, nk=nch_ * 128, hb=hb: e.tensor_copy(
                        out=arv(PST + hb * 640, 640)[:, 0:nk],
                        in_=ptr[:, 0:nk]), waits=[trs, stage_guard[hb]])
                    ptr_guard = mev
                    wpv = [mev]
                    if first_pv:
                        wpv.append(pv_guard[hb])
                    mpv = None
                    for j in range(nch_):
                        kc = c_lo_ + j
                        mpv = top(lambda e, j=j, kc=kc, hb=hb, qt=qt_, pv=pv,
                                  vws=vws, st=(qt_ == 3 and j == nch_ - 1),
                                  fp=(first_pv and j == 0), pr_=pend_pr[0]:
                                  e.matmul(
                            pv[0:65, qt * 128:(qt + 1) * 128],
                            arv(vws + kc * 520 + (2 * pr_ + hb) * 65, 65),
                            arv(PST + hb * 640 + j * 128, 128),
                            start=fp, stop=st),
                            waits=wpv if j == 0 else [])
                    stage_guard[hb] = mpv
                    last_mm = mpv

            pend_pr = [0]
            for pr in range(4):
                for qt in range(4):
                    ti = pr * 4 + qt
                    c_lo = max(kc_lo0, qt)
                    nch_ = qt + 5 - c_lo
                    w0 = c_lo * 128
                    nk = nch_ * 128
                    exps = [None, None]
                    pbufs = [None, None]
                    if qt == 2 and pend_norm is not None:
                        pend_norm()
                        pend_norm = None
                    for hb, duo in ((0, duo0), (1, duo1)):
                        prev_mex = last_exp[hb]
                        wl = [prev_mex]
                        if pr == 0 and qt == 0:
                            wl += mks
                        pieces = [(0, min(nk, 512))]
                        if nk > 512:
                            pieces.append((512, nk - 512))
                        first = True
                        mqk = None
                        for (o0, n_) in pieces:
                            mqk = top(lambda e, pr=pr, hb=hb, qt=qt, kws=kws,
                                      qws=qws, duo=duo, o0=o0, n_=n_, w0=w0:
                                      e.matmul(
                                duo[:, o0:o0 + n_],
                                AR[hb * 64:(hb + 1) * 64,
                                   qws + pr * 512 + qt * 128:qws + pr * 512 + (qt + 1) * 128],
                                AR[hb * 64:(hb + 1) * 64,
                                   kws + pr * 1024 + w0 + o0:kws + pr * 1024 + w0 + o0 + n_],
                                start=True, stop=False,
                                tile_position=(hb * 64, 0)), waits=wl if first else [])
                            first = False
                        if c_lo == qt:
                            top(lambda e, duo=duo: e.matmul(
                                duo[:, 0:128], idb128[:], crn_s[:],
                                start=False, stop=True))
                        mqk = top(lambda e, duo=duo, nk=nk: e.matmul(
                            duo[:, nk - 128:nk], idb128[:], crn_e[:],
                            start=False, stop=True))
                        mxc = MS[:, hb * 2 + ti % 2:hb * 2 + ti % 2 + 1]
                        mmx = vop(lambda e, duo=duo, nk=nk, mxc=mxc: e.tensor_reduce(
                            out=mxc, in_=duo[:, 0:nk], axis=AXX, op=ALU.max,
                            negate=True), waits=[mqk, prev_mex])
                        pbuf = arv(AR_OWL + 2048 + (hb * 2 + ti % 2) * 1024, 1024)
                        mex = aop(lambda e, duo=duo, nk=nk, pbuf=pbuf, mxc=mxc:
                                  e.activation(out=pbuf[:, 0:nk], in_=duo[:, 0:nk],
                                               func=AF.Exp, bias=mxc),
                                  waits=[mmx, mx_guard[hb * 2 + ti % 2]])
                        last_exp[hb] = mex
                        exps[hb] = mex
                        pbufs[hb] = pbuf
                    emit_back()
                    pend[0] = (qt, c_lo, nch_, exps, pbufs, qt == 0)
                    pend_pr[0] = pr
                emit_back()
                stg = 5120 + (pr % 2) * 512
                cA = vop(lambda e, stg=stg: e.tensor_copy(
                    out=SF[0:64, stg:stg + 512], in_=pA[0:64, :]),
                    waits=[last_mm])
                rA = vop(lambda e: e.tensor_scalar(
                    out=RAB[0:1, :], in0=pA[64:65, :],
                    scalar1=cst[0:1, CST_G38:CST_G38 + 1], scalar2=None, op0=ALU.add),
                    waits=[rab_guard])
                cB = vop(lambda e, stg=stg: e.tensor_copy(
                    out=SF[64:128, stg:stg + 512], in_=pB[0:64, :]))
                rB = vop(lambda e: e.tensor_scalar(
                    out=RAB[32:33, :], in0=pB[64:65, :],
                    scalar1=cst[32:33, CST_G38:CST_G38 + 1], scalar2=None, op0=ALU.add))
                pv_guard = [rA, rB]

                def mk_norm(pr=pr, w=w, stg=stg, rB=rB):
                    nonlocal pc_guard, rab_guard
                    rln = aop(lambda e: e.activation(
                        out=RAB[0:33, :], in_=RAB[0:33, :], func=AF.Ln),
                        waits=[rB])
                    rex = aop(lambda e: e.activation(
                        out=RAB[0:33, :], in_=RAB[0:33, :], func=AF.Exp, scale=-1.0))
                    mbc = top(lambda e: e.matmul(pC[:, 0:512], sel64[:], RAB[:],
                                                 start=True, stop=True),
                              waits=[rex, pc_guard])
                    rab_guard = mbc
                    o1 = vop(lambda e, pr=pr, stg=stg: e.tensor_tensor(
                        out=arv(AR_OWL + pr * 512, 512, 0, 64),
                        in0=SF[0:64, stg:stg + 512], in1=pC[0:64, :], op=ALU.mult),
                        waits=[mbc, B.now("dC")])
                    o2 = vop(lambda e, pr=pr, stg=stg: e.tensor_tensor(
                        out=arv(AR_OWL + pr * 512, 512, 64, 128),
                        in0=SF[64:128, stg:stg + 512], in1=pC[64:128, :], op=ALU.mult))
                    pc_guard = o2
                    B.dma("sync", olT[s][pr * 128:(pr + 1) * 128, w * 512:(w + 1) * 512],
                          arv(AR_OWL + pr * 512, 512), waits=[o1, o2], sem="dC")
                pend_norm = mk_norm
            win_guard[w % 2] = last_mm
        if pend_norm is not None:
            pend_norm()
            pend_norm = None
        B.barrier()

    def s5(s, l, srcT, dstT):
        O0, SRC, OUTS = 0, 4096, 5120
        o_guard = [None] * 4
        slot_guard = [None] * 4
        src_guard = [None, None]
        out_dma = [None, None]
        for c in range(NCH):
            c0 = c * 512
            mo = []
            for fi in range(4):
                mo.append(B.dma("sync", SB[:, O0 + fi * 512:O0 + (fi + 1) * 512],
                                olT[s][fi * 128:(fi + 1) * 128, c0:c0 + 512],
                                waits=[o_guard[fi]], sem="dA" if c % 2 == 0 else "dB"))
            for g in range(8):
                slot = DUO[g % 4]
                mons = None
                for fi in range(4):
                    w = (mo + [slot_guard[g % 4]]) if fi == 0 else []
                    mons = top(lambda e, g=g, fi=fi, slot=slot: e.matmul(
                        slot, SB[:, 2048 + fi * 1024 + g * 128:2048 + fi * 1024 + (g + 1) * 128],
                        SB[:, O0 + fi * 512:O0 + (fi + 1) * 512],
                        start=(fi == 0), stop=(fi == 3)), waits=w)
                for fi in range(4):
                    o_guard[fi] = mons
                st = SRC + (g % 2) * 512
                st2 = OUTS + (g % 2) * 512
                ms_ = B.dma("sync", SF[:, st:st + 512],
                            srcT[g * 128:(g + 1) * 128, c0:c0 + 512],
                            waits=[src_guard[g % 2]], sem="dE" if g % 2 == 0 else "dF")
                ma = vop(lambda e, slot=slot, st=st, st2=st2: e.tensor_tensor(
                    out=SF[:, st2:st2 + 512], in0=slot, in1=SF[:, st:st + 512],
                    op=ALU.add), waits=[mons, ms_, out_dma[g % 2]])
                slot_guard[g % 4] = ma
                src_guard[g % 2] = ma
                out_dma[g % 2] = B.dma(
                    "gpsimd", dstT[g * 128:(g + 1) * 128, c0:c0 + 512],
                    SF[:, st2:st2 + 512], waits=[ma],
                    sem="dC" if g % 2 == 0 else "dD")
        B.barrier()

    duo_guard = [None] * 4

    def s6(s, l, srcT, dstT, final):
        XH, SQ2, HA = 0, 4096, 5120
        MUB, RSB, GL = 4096, 4608, 5120
        xb_guard = [None] * 8
        sq_guard = [None, None]
        lnin_guard = [None, None, None]
        xh_guard = [None] * 8
        ha_guard = [None] * NI2
        res_guard = [None, None]
        dn_guard = [None, None]
        out_dma = [None, None]
        stat_guard = None
        fin_guard = [None] * 8
        x0_guard = [None, None]
        w2slot_guard = [None, None]
        hlast = None
        lnlast = None
        for c in range(NCH):
            c0 = c * 512
            mxb = []
            gsem = "gA" if c % 2 == 0 else "gB"
            if c == 0:
                for f in range(8):
                    dst = SB[:, XH + f * 512:XH + (f + 1) * 512]
                    if f >= 4:
                        mz = gop(lambda e, dst=dst: e.memset(dst[:, 0:1], 0.0),
                                 waits=[xb_guard[f]])
                        mxb.append(B.dma("gpsimd", dst[:, 1:512],
                                         srcT[f * 128:(f + 1) * 128, 0:511],
                                         waits=[mz], sem=gsem))
                    else:
                        mxb.append(B.dma("gpsimd", dst,
                                         srcT[f * 128:(f + 1) * 128, c0:c0 + 512],
                                         waits=[xb_guard[f]], sem=gsem))
            else:
                mxb.append(B.dma(
                    "gpsimd",
                    SB[:, XH:XH + 2048].rearrange("p (f t) -> p f t", t=512),
                    srcT[:, c0:c0 + 512].rearrange("(f p) t -> p f t", p=128)[:, 0:4],
                    waits=xb_guard[0:4], sem=gsem))
                mxb.append(B.dma(
                    "gpsimd",
                    SB[:, XH + 2048:XH + 4096].rearrange("p (f t) -> p f t", t=512),
                    srcT[:, c0 - 1:c0 + 511].rearrange("(f p) t -> p f t", p=128)[:, 4:8],
                    waits=xb_guard[4:8], sem=gsem))
            msum = None
            msumsq = None
            for f in range(8):
                sq = SB[:, SQ2 + (f % 2) * 512:SQ2 + (f % 2) * 512 + 512]
                msq = aop(lambda e, f=f, sq=sq: e.activation(
                    out=sq, in_=SB[:, XH + f * 512:XH + (f + 1) * 512],
                    func=AF.Square), waits=[mxb[-1], sq_guard[f % 2]])
                wsum = [mxb[-1]]
                if f == 0:
                    wsum += [stat_guard, w2slot_guard[0], w2slot_guard[1]]
                msum = top(lambda e, f=f: e.matmul(
                    pA[0:1, :], onescol[:], SB[:, XH + f * 512:XH + (f + 1) * 512],
                    start=(f == 0), stop=(f == 7)), waits=wsum)
                wsq = [msq]
                if f == 0:
                    wsq += [stat_guard, w2slot_guard[0], w2slot_guard[1]]
                msumsq = top(lambda e, f=f, sq=sq: e.matmul(
                    pB[0:1, :], onescol[:], sq, start=(f == 0), stop=(f == 7)),
                    waits=wsq)
                sq_guard[f % 2] = msumsq
            ms1 = vop(lambda e: e.tensor_scalar(out=MUV[0:1, :], in0=pA[0:1, :],
                                                scalar1=1.0 / D, scalar2=None,
                                                op0=ALU.mult), waits=[msum])
            vop(lambda e: e.tensor_tensor(out=RAB[32:33, :], in0=MUV[0:1, :],
                                          in1=MUV[0:1, :], op=ALU.mult))
            vop(lambda e: e.tensor_scalar(out=MUV[32:33, :], in0=pB[0:1, :],
                                          scalar1=1.0 / D, scalar2=None,
                                          op0=ALU.mult), waits=[msumsq])
            ms4 = vop(lambda e: e.tensor_tensor(out=MUV[32:33, :], in0=MUV[32:33, :],
                                                in1=RAB[32:33, :], op=ALU.subtract))
            aop(lambda e: e.activation(out=RAB[32:33, :], in_=MUV[32:33, :],
                                       func=AF.Ln, bias=cst[32:33, CST_EPS:CST_EPS + 1]),
                waits=[ms4])
            ms6 = aop(lambda e: e.activation(out=MUV[32:33, :], in_=RAB[32:33, :],
                                             func=AF.Exp, scale=-0.5))
            if "dbgmu" in dbg and l == 0:
                B.dma("sync", dbgmu[s][0, c], MUV[0:1, :], waits=[ms6], sem="dH")
                B.dma("sync", dbgmu[s][1, c], MUV[32:33, :], waits=[ms6], sem="dH")
            mb1 = top(lambda e: e.matmul(pA[:, 0:512], selM0[:], MUV[:],
                                         start=True, stop=True), waits=[ms6, ms1])
            mb2 = top(lambda e: e.matmul(pB[:, 0:512], selM32[:], MUV[:],
                                         start=True, stop=True))
            wme = [res_guard[0], res_guard[1], x0_guard[0], x0_guard[1]]
            me1 = vop(lambda e: e.tensor_copy(out=SF[:, MUB:MUB + 512],
                                              in_=pA[:, 0:512]), waits=[mb1] + wme)
            me2 = vop(lambda e: e.tensor_copy(out=SF[:, RSB:RSB + 512],
                                              in_=pB[:, 0:512]), waits=[mb2])
            stat_guard = me2
            for f in range(8):
                lt = SF[:, (f % 3) * 512:(f % 3) * 512 + 512]
                wlt = [lnin_guard[f % 3]]
                if f >= 4 and c == 0:
                    mzl = vop(lambda e, lt=lt: e.tensor_scalar(
                        out=lt[:, 0:1], in0=cst[:, CST_ONE:CST_ONE + 1],
                        scalar1=0.0, scalar2=None, op0=ALU.mult), waits=wlt)
                    mlt = B.dma("sync", lt[:, 1:512],
                                srcT[f * 128:(f + 1) * 128, 0:511], waits=[mzl],
                                sem="dA")
                elif f >= 4:
                    mlt = B.dma("sync", lt, srcT[f * 128:(f + 1) * 128, c0 - 1:c0 + 511],
                                waits=wlt, sem="dA")
                else:
                    mlt = B.dma("sync", lt, srcT[f * 128:(f + 1) * 128, c0:c0 + 512],
                                waits=wlt, sem="dA")
                vop(lambda e, lt=lt: e.tensor_tensor(
                    out=lt, in0=lt, in1=SF[:, MUB:MUB + 512], op=ALU.subtract),
                    waits=[mlt, me1])
                vop(lambda e, lt=lt: e.tensor_tensor(
                    out=lt, in0=lt, in1=SF[:, RSB:RSB + 512], op=ALU.mult),
                    waits=[me2])
                m3 = vop(lambda e, lt=lt, f=f: e.tensor_scalar(
                    out=SB[:, XH + f * 512:XH + (f + 1) * 512], in0=lt,
                    scalar1=vec_lng[:, l * 8 + f:l * 8 + f + 1],
                    scalar2=vec_lnb[:, l * 8 + f:l * 8 + f + 1],
                    op0=ALU.mult, op1=ALU.add),
                    waits=[msum, msumsq])
                lnin_guard[f % 3] = m3
                xh_guard[f] = m3
                lnlast = m3
                if "dbgxh" in dbg and l == 0:
                    xh_guard[f] = B.dma(
                        "sync", dbgxh[s][f * 128:(f + 1) * 128, c0:c0 + 512],
                        SB[:, XH + f * 512:XH + (f + 1) * 512], waits=[m3], sem="dG")
            for j in range(NI2):
                sa_ = DUO[(2 * j) % 4]
                sg_ = DUO[(2 * j + 1) % 4]
                ma = None
                for fi in range(8):
                    wa = [xh_guard[fi]]
                    if fi == 0:
                        wa.append(duo_guard[(2 * j) % 4])
                    ma = top(lambda e, fi=fi, j=j, sa_=sa_: e.matmul(
                        sa_, arv(AR_W1 + fi * 5632 + (2 * j) * 128, 128),
                        SB[:, XH + fi * 512:XH + (fi + 1) * 512],
                        start=(fi == 0), stop=(fi == 7)), waits=wa)
                mg = None
                for fi in range(8):
                    wg_ = [duo_guard[(2 * j + 1) % 4]] if fi == 0 else []
                    mg = top(lambda e, fi=fi, j=j, sg_=sg_: e.matmul(
                        sg_, arv(AR_W1 + fi * 5632 + (2 * j + 1) * 128, 128),
                        SB[:, XH + fi * 512:XH + (fi + 1) * 512],
                        start=(fi == 0), stop=(fi == 7)), waits=wg_)
                gl = SF[:, GL + (j % 2) * 512:GL + (j % 2) * 512 + 512]
                wgl = [mg, ha_guard_src[j % 2], dn_guard[j % 2], out_dma[j % 2]]
                mgl = aop(lambda e, gl=gl, sg_=sg_, j=j: e.activation(
                    out=gl, in_=sg_, func=AF.Gelu,
                    bias=vec_b1[:, l * NF1 + 2 * j + 1:l * NF1 + 2 * j + 2]),
                    waits=wgl)
                duo_guard[(2 * j + 1) % 4] = mgl
                mtmp = vop(lambda e, j=j, sa_=sa_: e.tensor_scalar(
                    out=SF[:, 6144:6656], in0=sa_,
                    scalar1=vec_b1[:, l * NF1 + 2 * j:l * NF1 + 2 * j + 1],
                    scalar2=None, op0=ALU.add), waits=[ma])
                mh = vop(lambda e, j=j, gl=gl: e.tensor_tensor(
                    out=SB[:, HA + j * 512:HA + (j + 1) * 512], in0=SF[:, 6144:6656],
                    in1=gl, op=ALU.mult),
                    waits=[mgl, ha_guard[j]])
                duo_guard[(2 * j) % 4] = mh
                ha_guard_src[j % 2] = mh
                hlast = mh
                if "dbghg" in dbg and l == 0:
                    m_ = B.dma(
                        "sync", dbghg[s][j * 128:(j + 1) * 128, c0:c0 + 512],
                        SB[:, HA + j * 512:HA + (j + 1) * 512], waits=[mh], sem="dG")
                    if "dbgsync" in dbg:
                        hlast = m_
                        ha_guard_src[j % 2] = m_
            for f in range(8):
                xb_guard[f] = mg
            for fo in range(8):
                bank = pA if fo % 2 == 0 else pB
                mw = None
                for fi in range(NI2):
                    ww = []
                    if fi == 0:
                        ww = [w2slot_guard[fo % 2], hlast, ms1 if fo == 0 else None]
                    mw = top(lambda e, fi=fi, fo=fo, bank=bank: e.matmul(
                        bank[:, 0:512], arv(AR_W2 + fi * 1024 + fo * 128, 128),
                        SB[:, HA + fi * 512:HA + (fi + 1) * 512],
                        start=(fi == 0), stop=(fi == NI2 - 1)), waits=ww)
                for fi in range(NI2):
                    ha_guard[fi] = mw
                if not final:
                    st = 4096 + (fo % 2) * 512
                    st2 = 5120 + (fo % 2) * 512
                    msr = B.dma("sync", SF[:, st:st + 512],
                                srcT[fo * 128:(fo + 1) * 128, c0:c0 + 512],
                                waits=[res_guard[fo % 2], lnlast], sem="dB")
                    mv1 = vop(lambda e, bank=bank, st2=st2, fo=fo: e.tensor_scalar(
                        out=SF[:, st2:st2 + 512], in0=bank[:, 0:512],
                        scalar1=vec_b2[:, l * 8 + fo:l * 8 + fo + 1],
                        scalar2=None, op0=ALU.add),
                        waits=[mw, out_dma[fo % 2], hlast])
                    mv2 = vop(lambda e, st=st, st2=st2: e.tensor_tensor(
                        out=SF[:, st2:st2 + 512], in0=SF[:, st2:st2 + 512],
                        in1=SF[:, st:st + 512], op=ALU.add), waits=[msr])
                    w2slot_guard[fo % 2] = mv2
                    res_guard[fo % 2] = mv2
                    out_dma[fo % 2] = B.dma(
                        "sync", dstT[fo * 128:(fo + 1) * 128, c0:c0 + 512],
                        SF[:, st2:st2 + 512], waits=[mv2],
                        sem="dD" if fo % 2 == 0 else "dE")
                else:
                    st = 4096 + (fo % 2) * 512
                    st3 = 5120 + (fo % 2) * 512
                    msr = B.dma("sync", SF[:, st:st + 512],
                                srcT[fo * 128:(fo + 1) * 128, c0:c0 + 512],
                                waits=[res_guard[fo % 2], lnlast], sem="dB")
                    mdn = B.dma("sync", SF[:, st3:st3 + 512],
                                dnT[s][fo * 128:(fo + 1) * 128, c0:c0 + 512],
                                waits=[dn_guard[fo % 2], hlast], sem="dC")
                    ft = SF[:, fo * 512:(fo + 1) * 512]
                    mv1 = vop(lambda e, bank=bank, ft=ft, fo=fo: e.tensor_scalar(
                        out=ft, in0=bank[:, 0:512],
                        scalar1=vec_b2[:, l * 8 + fo:l * 8 + fo + 1],
                        scalar2=None, op0=ALU.add),
                        waits=[mw, fin_guard[fo]] + lnin_guard)
                    mv2 = vop(lambda e, ft=ft, st=st: e.tensor_tensor(
                        out=ft, in0=ft, in1=SF[:, st:st + 512], op=ALU.add),
                        waits=[msr])
                    mv3 = vop(lambda e, ft=ft, st3=st3: e.tensor_tensor(
                        out=ft, in0=ft, in1=SF[:, st3:st3 + 512], op=ALU.add),
                        waits=[mdn])
                    w2slot_guard[fo % 2] = mv3
                    res_guard[fo % 2] = mv3
                    dn_guard[fo % 2] = mv3
                    fin_guard[fo] = mv3
            if final:
                mt_last = None
                for grp in range(4):
                    xs = 4096 + (grp % 2) * 1024
                    mx0 = B.dma("sync", SF[:, xs:xs + 1024],
                                x0[s, c0 + grp * 128:c0 + (grp + 1) * 128, :],
                                waits=[x0_guard[grp % 2], fin_guard[7]], sem="dF")
                    madds = []
                    for f in range(8):
                        slot = DUO[f % 4]
                        mt = top(lambda e, f=f, grp=grp, slot=slot: e.transpose(
                            slot[:, 0:128],
                            SF[:, f * 512 + grp * 128:f * 512 + (grp + 1) * 128],
                            idf[:]),
                            waits=[fin_guard[f], duo_guard[f % 4]])
                        mt_last = mt
                        mad = vop(lambda e, f=f, xs=xs, slot=slot: e.tensor_tensor(
                            out=SF[:, xs + f * 128:xs + (f + 1) * 128],
                            in0=slot[:, 0:128],
                            in1=SF[:, xs + f * 128:xs + (f + 1) * 128],
                            op=ALU.add), waits=[mt, mx0])
                        duo_guard[f % 4] = mad
                        madds.append(mad)
                    x0_guard[grp % 2] = B.dma(
                        "sync", out[s, c0 + grp * 128:c0 + (grp + 1) * 128, :],
                        SF[:, xs:xs + 1024], waits=madds,
                        sem="dG" if grp % 2 == 0 else "dH")
                for f in range(8):
                    fin_guard[f] = [mt_last, x0_guard[1]]
                for i in range(3):
                    lnin_guard[i] = mt_last
                res_guard = [x0_guard[0], x0_guard[1]]
                dn_guard = [x0_guard[0], x0_guard[1]]
        B.barrier()

    ha_guard_src = [None, None]

    ph0()
    fold_consts(0)
    for s_ in range(S):
        s1a(s_)
    if phases >= 2:
        for s_ in range(S):
            s1b_dense(s_)
    if phases >= 3:
        for s_ in range(S):
            s1b_local(s_, 0, xnT[s_])
    if phases >= 4:
        if phases >= 5:
            s2(0)
            g3 = s3_gen(0)
            s2(1, tick=lambda: next(g3, None))
            for _ in g3:
                pass
            s3(1)
        else:
            for s_ in range(S):
                s2(s_)
    def w12_early(l):
        B.dma("gpsimd", arv(AR_W1, 3 * 5632).rearrange("p (i o) -> p i o", o=5632),
              w1p[l].rearrange("(i p) o -> p i o", p=128)[:, 0:3])
        B.dma("gpsimd", arv(AR_W2 + 5 * 1024, 17 * 1024).rearrange(
            "p (i o) -> p i o", o=1024),
            w2p[l].rearrange("(i p) o -> p i o", p=128)[:, 5:22])

    def w12_rest(l):
        B.dma("gpsimd", arv(AR_W1 + 3 * 5632, 5 * 5632).rearrange(
            "p (i o) -> p i o", o=5632),
            w1p[l].rearrange("(i p) o -> p i o", p=128)[:, 3:8])
        B.dma("gpsimd", arv(AR_W2, 5 * 1024).rearrange("p (i o) -> p i o", o=1024),
              w2p[l].rearrange("(i p) o -> p i o", p=128)[:, 0:5])

    if phases >= 6:
        B.dma("gpsimd", SB[:, 2048:6144].rearrange("p (i o) -> p i o", o=1024),
              wo_w[0].rearrange("(i p) o -> p i o", p=128))
        B.barrier()
        if phases >= 7:
            w12_early(0)
        for s_ in range(S):
            s4(s_, 0)
        B.barrier()
        if phases >= 7:
            w12_rest(0)
        for s_ in range(S):
            s5(s_, 0, xnT[s_], xaT[s_])
    if phases >= 7:
        B.barrier()
        for s_ in range(S):
            s6(s_, 0, xaT[s_], xbT[s_], final=False)
    if phases >= 8:
        fold_consts(1)
        B.dma("gpsimd", arv(AR_WQL, 12288).rearrange("p (i o) -> p i o", o=1536),
              wqkv_l[1].rearrange("(i p) o -> p i o", p=128))
        B.barrier()
        for s_ in range(S):
            s1b_local(s_, 1, xbT[s_])
    if phases >= 9:
        B.dma("gpsimd", SB[:, 2048:6144].rearrange("p (i o) -> p i o", o=1024),
              wo_w[1].rearrange("(i p) o -> p i o", p=128))
        B.barrier()
        if phases >= 10:
            w12_early(1)
        for s_ in range(S):
            s4(s_, 1)
        B.barrier()
        if phases >= 10:
            w12_rest(1)
        for s_ in range(S):
            s5(s_, 1, xbT[s_], xaT[s_])
    if phases >= 10:
        B.barrier()
        for s_ in range(S):
            s6(s_, 1, xaT[s_], None, final=True)
    B.wait("sync", [B.now("sync")])
    B.flush()
    es.close()
    nc._bld_sems = {k: v[1] for k, v in B.sem.items()}
    return nc


def _host_layout(inputs):
    f32 = np.float32
    w1 = np.asarray(inputs["L0_ffn_w1"], f32)
    a = np.concatenate([w1[:, :, :HID], np.zeros((2, D, HIDP - HID), f32)], 2)
    g = np.concatenate([w1[:, :, HID:], np.zeros((2, D, HIDP - HID), f32)], 2)
    at = a.reshape(2, D, NI2, 128)
    gt = g.reshape(2, D, NI2, 128)
    w1p_ = np.stack([at, gt], axis=3).reshape(2, D, 2 * HIDP)
    b1 = np.asarray(inputs["L0_ffn_b1"], f32)
    b1a = np.concatenate([b1[:, :HID], np.zeros((2, HIDP - HID), f32)], 1)
    b1g = np.concatenate([b1[:, HID:], np.zeros((2, HIDP - HID), f32)], 1)
    b1t = np.stack([b1a.reshape(2, NI2, 128), b1g.reshape(2, NI2, 128)],
                   axis=2).reshape(2, NF1, 128).transpose(0, 2, 1)
    w2 = np.asarray(inputs["L0_ffn_w2"], f32)
    w2p_ = np.concatenate([w2, np.zeros((2, HIDP - HID, D), f32)], 1)
    qs = np.asarray(inputs["L0_ffn_q_scale"], f32)
    ks = np.asarray(inputs["L0_ffn_k_scale"], f32)
    qsw = np.concatenate([qs[:, 32:], qs[:, :32]], 1)
    ksw = np.concatenate([ks[:, 32:], ks[:, :32]], 1)
    return {
        "wqkv_d": np.ascontiguousarray(np.asarray(inputs["L0_attn_w_qkv"], f32)),
        "wproj": np.ascontiguousarray(np.asarray(inputs["L0_attn_w_proj"], f32)),
        "bprojT": np.ascontiguousarray(
            np.asarray(inputs["L0_attn_b_proj"], f32).reshape(8, 128).T),
        "wqkv_l": np.ascontiguousarray(np.asarray(inputs["L0_ffn_wqkv"], f32)),
        "qsT": np.ascontiguousarray(qs[:, :, None]),
        "ksT": np.ascontiguousarray(ks[:, :, None]),
        "qswT": np.ascontiguousarray(qsw[:, :, None]),
        "kswT": np.ascontiguousarray(ksw[:, :, None]),
        "wo_w": np.ascontiguousarray(np.asarray(inputs["L0_ffn_wo"], f32)),
        "lngT": np.ascontiguousarray(
            np.asarray(inputs["L0_ffn_ln_g"], f32).reshape(2, 8, 128).transpose(0, 2, 1)),
        "lnbT": np.ascontiguousarray(
            np.asarray(inputs["L0_ffn_ln_b"], f32).reshape(2, 8, 128).transpose(0, 2, 1)),
        "w1p": np.ascontiguousarray(w1p_),
        "b1pT": np.ascontiguousarray(b1t),
        "w2p": np.ascontiguousarray(w2p_),
        "b2T": np.ascontiguousarray(
            np.asarray(inputs["L0_ffn_b2"], f32).reshape(2, 8, 128).transpose(0, 2, 1)),
    }


def kernel(**inputs):
    x = np.ascontiguousarray(np.asarray(inputs["L0_attn_x"], np.float32))
    com = _host_layout(inputs)
    nc = build_program()
    in_maps = []
    for c in range(8):
        m = dict(com)
        m["x0"] = np.ascontiguousarray(x[2 * c:2 * c + 2])
        in_maps.append(m)
    res = run_bass_kernel_spmd(nc, in_maps, core_ids=list(range(8)))
    return np.concatenate([res.results[c]["out"] for c in range(8)], axis=0)
